# version 10
# baseline (speedup 1.0000x reference)
import numpy as np
import contextlib
import concourse.bass as bass
import concourse.mybir as mybir
from concourse.bass import IndirectOffsetOnAxis
from concourse.bass_utils import run_bass_kernel_spmd

F32 = mybir.dt.float32
BF16 = mybir.dt.bfloat16
I32 = mybir.dt.int32
AF = mybir.ActivationFunctionType
ALU = mybir.AluOpType
AX = mybir.AxisListType.X

D, NCH, NT, S, HPC = 4096, 32, 1024, 4096, 8
E, R, GH = 8, 16, 32
CAP, ALPHA = 3072.0, 1.0 / 16
SCALE = float(128) ** -0.5
LN_EPS = 1e-5
NEG_BIG = -1.0e30

RW_MU, RW_RS, RW_V, RW_MX = 0, NT, 2 * NT, 3 * NT
RW_KS, RW_QN = 4096, 8192 - 4096
RW_KS = 4096
RW_CPP = 0
RW_DNV = 0
RW_QNR = 4096

RO_LN, RO_G, RO_ALW = 0, 2 * NT, 3 * NT
RO_CPP, RO_DNV = 4096, 8192


class Ctr:
    __slots__ = ("sem", "n")


def build_program():
    nc = bass.Bass("TRN2", target_bir_lowering=False)

    xT = nc.dram_tensor("xT", [D, NT], F32, kind="ExternalInput").ap()
    wqkvT = nc.dram_tensor("wqkvT", [D, 3 * NT], F32, kind="ExternalInput").ap()
    woutT = nc.dram_tensor("woutT", [D, D], F32, kind="ExternalInput").ap()
    ffnWT = nc.dram_tensor("ffnWT", [D, D], F32, kind="ExternalInput").ap()
    loraAT = nc.dram_tensor("loraAT", [D, E * R], F32, kind="ExternalInput").ap()
    loraBs = nc.dram_tensor("loraBs", [E * R, D], F32, kind="ExternalInput").ap()
    gw1T = nc.dram_tensor("gw1T", [D, GH], F32, kind="ExternalInput").ap()
    gw2T = nc.dram_tensor("gw2T", [GH, E], F32, kind="ExternalInput").ap()
    qkln = nc.dram_tensor("qkln", [128, 32], F32, kind="ExternalInput").ap()
    gvec = nc.dram_tensor("gvec", [GH, 3], F32, kind="ExternalInput").ap()
    gb2r = nc.dram_tensor("gb2r", [1, E], F32, kind="ExternalInput").ap()
    alibik = nc.dram_tensor("alibik", [128, HPC * NCH], F32,
                            kind="ExternalInput").ap()
    alibiq = nc.dram_tensor("alibiq", [1, HPC * S], F32,
                            kind="ExternalInput").ap()
    oidx = nc.dram_tensor("oidx", [128, NCH], I32, kind="ExternalInput").ap()

    outT = nc.dram_tensor("outT", [D, NT], F32, kind="ExternalOutput").ap()

    xres1 = nc.dram_tensor("xres1", [D, NT], F32, kind="Internal").ap()
    xres2 = nc.dram_tensor("xres2", [D, NT], F32, kind="Internal").ap()
    _YG = [4, 4, 8, 8, 8]
    agy_i = [nc.dram_tensor(f"agy_i{a}", [128 * _YG[a], NT], BF16,
                            kind="Internal").ap() for a in range(5)]
    agy_o = [nc.dram_tensor(f"agy_o{a}", [4 * 128 * _YG[a], NT], BF16,
                            kind="Internal").ap() for a in range(5)]
    k_raw = nc.dram_tensor("k_raw", [NT, S], BF16, kind="Internal").ap()
    q_raw = nc.dram_tensor("q_raw", [NT, S], BF16, kind="Internal").ap()
    v_scr = nc.dram_tensor("v_scr", [NT, S], BF16, kind="Internal").ap()
    ka_scr = nc.dram_tensor("ka_scr", [NT, S], BF16, kind="Internal").ap()
    qa_scr = nc.dram_tensor("qa_scr", [NT, S], BF16, kind="Internal").ap()
    arki = nc.dram_tensor("arki", [1, 8192], F32, kind="Internal").ap()
    arko = nc.dram_tensor("arko", [1, 8192], F32, kind="Internal").ap()
    arqi = nc.dram_tensor("arqi", [1, 8192], F32, kind="Internal").ap()
    arqo = nc.dram_tensor("arqo", [1, 8192], F32, kind="Internal").ap()
    aoiA = nc.dram_tensor("aoiA", [7 * 128, S], BF16, kind="Internal").ap()
    aoiB = nc.dram_tensor("aoiB", [1 * 128, S], BF16, kind="Internal").ap()
    aooA = nc.dram_tensor("aooA", [28 * 128, S], BF16, kind="Internal").ap()
    aooB = nc.dram_tensor("aooB", [4 * 128, S], BF16, kind="Internal").ap()
    cin = [nc.dram_tensor(f"cin{i}", [1, 16], F32, kind="Internal").ap()
           for i in range(2)]
    cout = [nc.dram_tensor(f"cout{i}", [1, 16], F32, kind="Internal").ap()
            for i in range(2)]
    rsc = nc.dram_tensor("rsc", [1, 16384], F32, kind="Internal").ap()

    es = contextlib.ExitStack()
    with es:
        ident = es.enter_context(nc.sbuf_tensor("ident", [128, 128], BF16))
        mtri = es.enter_context(nc.sbuf_tensor("mtri", [128, 128], F32))
        ones_bc = es.enter_context(nc.sbuf_tensor("ones_bc", [128, 1], BF16))
        ones_r = es.enter_context(nc.sbuf_tensor("ones_r", [1, 128], BF16))
        ones_fc = es.enter_context(nc.sbuf_tensor("ones_fc", [128, 1], F32))
        ones32f = es.enter_context(nc.sbuf_tensor("ones32f", [GH, 1], F32))
        ones32b = es.enter_context(nc.sbuf_tensor("ones32b", [GH, 1], BF16))
        qkln_sb = es.enter_context(nc.sbuf_tensor("qkln_sb", [128, 32], F32))
        gvec_sb = es.enter_context(nc.sbuf_tensor("gvec_sb", [GH, 3], F32))
        gb2B = es.enter_context(nc.sbuf_tensor("gb2B", [128, E], F32))
        gw1_sb = es.enter_context(nc.sbuf_tensor("gw1_sb", [128, NCH * GH], BF16))
        gw2_sb = es.enter_context(nc.sbuf_tensor("gw2_sb", [GH, E], BF16))
        repl_sb = es.enter_context(nc.sbuf_tensor("repl_sb", [E, 128], BF16))
        lA_sb = es.enter_context(nc.sbuf_tensor("lA_sb", [128, NCH * 128], BF16))
        lB_sb = es.enter_context(nc.sbuf_tensor("lB_sb", [128, D], BF16))
        alibik_sb = es.enter_context(
            nc.sbuf_tensor("alibik_sb", [128, HPC * NCH], F32))
        oidx_sb = es.enter_context(nc.sbuf_tensor("oidx_sb", [128, NCH], I32))
        bmu = es.enter_context(nc.sbuf_tensor("bmu", [128, NT], F32))
        brstd = es.enter_context(nc.sbuf_tensor("brstd", [128, NT], F32))
        rows = es.enter_context(nc.sbuf_tensor("rows", [1, 8192], F32))
        epsc = es.enter_context(nc.sbuf_tensor("epsc", [1, 1], F32))

        def C(name):
            c = Ctr()
            c.sem = es.enter_context(nc.semaphore(name))
            c.n = 0
            return c

        c_set, c_pre = C("c_set"), C("c_pre")
        c_x, c_xu = C("c_x"), C("c_xu")
        c_wl, c_wu = C("c_wl"), C("c_wu")
        c_tbu = C("c_tbu")
        c_mm, c_ev, c_ex = C("c_mm"), C("c_ev"), C("c_ex")
        c_row, c_sc, c_cc, c_gl = C("c_row"), C("c_sc"), C("c_cc"), C("c_gl")
        c_amm, c_asu, c_aex = C("c_amm"), C("c_asu"), C("c_aex")
        c_apv, c_aac, c_aon, c_adn = C("c_apv"), C("c_aac"), C("c_aon"), C("c_adn")
        c_msk = C("c_msk")
        sx = [C(f"sx{i}") for i in range(3)]
        stb = [C(f"stb{i}") for i in range(2)]
        swl = [C(f"swl{i}") for i in range(2)]
        sst = [C(f"sst{i}") for i in range(2)]
        ssd = [C(f"ssd{i}") for i in range(2)]
        skh = [C(f"skh{i}") for i in range(2)]
        sqh = [C(f"sqh{i}") for i in range(2)]
        svt = C("svt")
        sob = [C(f"sob{i}") for i in range(2)]
        sdv = [C(f"sdv{i}") for i in range(2)]
        sdb = [C(f"sdb{i}") for i in range(2)]
        s_aq, s_cpb, s_cbb = C("s_aq"), C("s_cpb"), C("s_cbb")
        s_lnr, s_cnt = C("s_lnr"), C("s_cnt")
        sqa = [C(f"sqa{i}") for i in range(2)]
        sya = [C(f"sya{i}") for i in range(2)]
        c_ybr = C("c_ybr")
        s_bnc = C("s_bnc")
        syb = [C(f"syb{i}") for i in range(5)]
        s_ar = C("s_ar")

        def slot_tok(group, base, k, idx):
            s = group[idx % k]
            return s.sem, base[idx % k] + 16 * (idx // k + 1)

        def slot_bases(group):
            return [s.n for s in group]

        def slot_bump(group, k, nuses):
            for i in range(nuses):
                group[i % k].n += 16

        block = es.enter_context(nc.Block())

        @block.gpsimd
        def _(g):
            for dst, src in [
                (qkln_sb[:, :], qkln), (gvec_sb[:, :], gvec),
                (gw2_sb[:, :], gw2T), (lB_sb[:, :], loraBs),
                (alibik_sb[:, :], alibik),
                (oidx_sb[:, :], oidx),
            ]:
                g.dma_start(out=dst, in_=src).then_inc(c_set.sem, 16)
            g.dma_start(
                out=gw1_sb[:, :].rearrange("p (c gh) -> p c gh", c=NCH),
                in_=gw1T.rearrange("(c p) gh -> p c gh", p=128)
            ).then_inc(c_set.sem, 16)
            g.dma_start(
                out=lA_sb[:, :].rearrange("p (c f) -> p c f", c=NCH),
                in_=loraAT.rearrange("(c p) f -> p c f", p=128)
            ).then_inc(c_set.sem, 16)
            g.dma_start(out=gb2B[:, :],
                        in_=gb2r.partition_broadcast(128)).then_inc(c_set.sem, 16)
            g.memset(ident[:, :], 1.0).then_inc(c_pre.sem, 1)
            g.memset(mtri[:, :], 0.0).then_inc(c_pre.sem, 1)
            g.memset(ones_bc[:, :], 1.0).then_inc(c_pre.sem, 1)
            g.memset(ones_r[:, :], 1.0).then_inc(c_pre.sem, 1)
            g.memset(ones_fc[:, :], 1.0).then_inc(c_pre.sem, 1)
            g.memset(ones32f[:, :], 1.0).then_inc(c_pre.sem, 1)
            g.memset(ones32b[:, :], 1.0).then_inc(c_pre.sem, 1)
            g.memset(repl_sb[:, :], 1.0).then_inc(c_pre.sem, 1)
            g.memset(epsc[:, :], LN_EPS).then_inc(c_pre.sem, 1)
            g.wait_ge(c_pre.sem, 9)
            g.affine_select(ident[:, :], ident[:, :], pattern=[[-1, 128]],
                            compare_op=ALU.is_equal, fill=0.0, base=0,
                            channel_multiplier=1).then_inc(c_pre.sem, 1)
            g.affine_select(mtri[:, :], mtri[:, :], pattern=[[1, 128]],
                            compare_op=ALU.is_ge, fill=NEG_BIG, base=0,
                            channel_multiplier=-1).then_inc(c_pre.sem, 1)
            g.affine_select(repl_sb[:, :], repl_sb[:, :], pattern=[[1, 128]],
                            compare_op=ALU.is_ge, fill=0.0, base=0,
                            channel_multiplier=-16).then_inc(c_pre.sem, 1)
            g.wait_ge(c_pre.sem, 12)
            g.affine_select(repl_sb[:, :], repl_sb[:, :], pattern=[[-1, 128]],
                            compare_op=ALU.is_ge, fill=0.0, base=15,
                            channel_multiplier=16).then_inc(c_pre.sem, 1)
        c_set.n += 16 * 9
        c_pre.n += 13
        T_SET, T_PRE = c_set.n, c_pre.n

        def ln_finalize(ps1, ps2, mm_tok, denom, roff, dst_mu, dst_rstd, nbc,
                        extra_bcast_ev_wait=None):
            evb = c_ev.n
            prev_bounce = s_lnr.n

            @block.vector
            def _(v):
                v.wait_ge(mm_tok[0], mm_tok[1])
                v.wait_ge(s_lnr.sem, prev_bounce)
                v.tensor_scalar_mul(rows[0:1, RW_MU:RW_MU + NT], ps1[0:1, :],
                                    1.0 / denom).then_inc(c_ev.sem, 1)
                v.wait_ge(c_ev.sem, evb + 1)
                v.tensor_mul(rows[0:1, RW_V:RW_V + NT],
                             rows[0:1, RW_MU:RW_MU + NT],
                             rows[0:1, RW_MU:RW_MU + NT]).then_inc(c_ev.sem, 1)
                v.wait_ge(c_ev.sem, evb + 2)
                v.scalar_tensor_tensor(
                    rows[0:1, RW_V:RW_V + NT], ps2[0:1, :], 1.0 / denom,
                    rows[0:1, RW_V:RW_V + NT], ALU.mult, ALU.subtract) \
                    .then_inc(c_ev.sem, 1)
            c_ev.n += 3
            t_var = c_ev.n

            @block.scalar
            def _(sc):
                sc.wait_ge(c_ev.sem, t_var)
                sc.activation(rows[0:1, RW_RS:RW_RS + NT],
                              rows[0:1, RW_V:RW_V + NT],
                              AF.Sqrt, bias=epsc[0:1, 0:1]).then_inc(c_ex.sem, 1)
            c_ex.n += 1
            t_sq = c_ex.n

            @block.vector
            def _(v):
                v.wait_ge(c_ex.sem, t_sq)
                v.reciprocal(rows[0:1, RW_RS:RW_RS + NT],
                             rows[0:1, RW_RS:RW_RS + NT]).then_inc(c_ev.sem, 1)
            c_ev.n += 1
            t_rec = c_ev.n

            rb = s_lnr.n

            @block.sync
            def _(s):
                s.wait_ge(c_ev.sem, t_rec)
                if extra_bcast_ev_wait is not None:
                    s.wait_ge(c_ev.sem, extra_bcast_ev_wait)
                s.dma_start(out=rsc[0:1, roff:roff + 2 * NT],
                            in_=rows[0:1, RW_MU:RW_MU + 2 * NT]) \
                    .then_inc(s_lnr.sem, 16)
                s.wait_ge(s_lnr.sem, rb + 16)
                s.dma_start(out=dst_mu,
                            in_=rsc[0:1, roff:roff + NT]
                            .partition_broadcast(nbc)).then_inc(s_lnr.sem, 16)
                s.dma_start(out=dst_rstd,
                            in_=rsc[0:1, roff + NT:roff + 2 * NT]
                            .partition_broadcast(nbc)).then_inc(s_lnr.sem, 16)
            s_lnr.n += 48
            return s_lnr.n, t_var

        ALL_CTRS = [c_set, c_pre, c_tbu, c_mm, c_ev, c_ex, c_cc, c_gl,
                    c_amm, c_asu, c_aex, c_apv, c_aac, c_aon, c_adn, c_msk,
                    s_lnr, s_cnt, s_aq, s_cpb, s_cbb, svt, s_ar] + \
            sx + stb + swl + sst + ssd + skh + sqh + sob + sdv + sdb + \
            sqa + sya + syb + [s_bnc, c_ybr]

        def phase_fence():
            vals = [(c.sem, c.n) for c in ALL_CTRS if c.n > 0]

            @block.gpsimd
            def _(g):
                for sm, vv in vals:
                    g.wait_ge(sm, vv)

            @block.tensor
            def _(t):
                for sm, vv in vals:
                    t.wait_ge(sm, vv)

            @block.vector
            def _(v):
                for sm, vv in vals:
                    v.wait_ge(sm, vv)

            @block.scalar
            def _(sc):
                for sm, vv in vals:
                    sc.wait_ge(sm, vv)

            @block.sync
            def _(s):
                for sm, vv in vals:
                    s.wait_ge(sm, vv)

        def emit_ffn(src, dst, cidx, next_stats, tag):
            ph = contextlib.ExitStack()
            with ph:
                xfb = ph.enter_context(
                    nc.sbuf_tensor(f"xf{tag}", [128, NCH * NT], BF16))
                xin = [ph.enter_context(
                    nc.sbuf_tensor(f"xi{i}{tag}", [128, NT], F32))
                    for i in range(3)]
                xo_sb = [ph.enter_context(
                    nc.sbuf_tensor(f"xq{i}{tag}", [128, NT], F32))
                    for i in range(2)]
                tb = [ph.enter_context(
                    nc.sbuf_tensor(f"tc{i}{tag}", [128, NT], BF16))
                    for i in range(4)]
                wslab = [ph.enter_context(
                    nc.sbuf_tensor(f"wb{i}{tag}", [128, NCH * 128], BF16))
                    for i in range(2)]
                h_sb = ph.enter_context(nc.sbuf_tensor(f"h{tag}", [GH, NT], F32))
                h2_sb = ph.enter_context(nc.sbuf_tensor(f"h2{tag}", [GH, NT], F32))
                hb_sb = ph.enter_context(
                    nc.sbuf_tensor(f"hb{tag}", [GH, 2 * NT], BF16))
                hr_sb = ph.enter_context(nc.sbuf_tensor(f"hr{tag}", [GH, NT], BF16))
                gbmu = ph.enter_context(nc.sbuf_tensor(f"gm{tag}", [GH, NT], F32))
                gbrs = ph.enter_context(nc.sbuf_tensor(f"gr{tag}", [GH, NT], F32))
                gat = ph.enter_context(nc.sbuf_tensor(f"ga{tag}", [128, 8 * E], F32))
                gng = ph.enter_context(nc.sbuf_tensor(f"gg{tag}", [128, E], F32))
                msk = ph.enter_context(
                    nc.sbuf_tensor(f"mk{tag}", [128, 8 * 2 * E], F32))
                mcl = ph.enter_context(nc.sbuf_tensor(f"mc{tag}", [128, 32], F32))
                w_sb = ph.enter_context(nc.sbuf_tensor(f"w{tag}", [128, 8 * E], BF16))
                wT_sb = ph.enter_context(nc.sbuf_tensor(f"wt{tag}", [E, NT], BF16))
                cnt_sb = ph.enter_context(nc.sbuf_tensor(f"ct{tag}", [1, 16], F32))
                alw_sb = ph.enter_context(nc.sbuf_tensor(f"aw{tag}", [128, 16], F32))
                tmp_sb = ph.enter_context(nc.sbuf_tensor(f"tp{tag}", [128, NT], F32))
                tmps = ph.enter_context(nc.sbuf_tensor(f"tz{tag}", [128, NT], BF16))
                ps_s1 = ph.enter_context(nc.psum_tensor(f"p1{tag}", [1, NT], F32))
                ps_s2 = ph.enter_context(nc.psum_tensor(f"p2{tag}", [1, NT], F32))

                xub, tbu = c_xu.n, c_tbu.n
                exb = c_ex.n
                sxb = slot_bases(sx)
                ssd_tot = [(s.sem, s.n) for s in ssd]

                @block.sync
                def _(s):
                    s.wait_ge(c_pre.sem, T_PRE)
                    for sm, vv in ssd_tot:
                        s.wait_ge(sm, vv)
                    for kc in range(NCH):
                        if kc >= 3:
                            s.wait_ge(c_tbu.sem, tbu + kc - 2)
                        s.dma_start(out=xin[kc % 3][:, :],
                                    in_=src[128 * kc:128 * (kc + 1), :]) \
                            .then_inc(sx[kc % 3].sem, 16)
                slot_bump(sx, 3, NCH)

                @block.scalar
                def _(sc):
                    sc.wait_ge(c_set.sem, T_SET)
                    for kc in range(NCH):
                        sc.wait_ge(*slot_tok(sx, sxb, 3, kc))
                        if kc >= 2:
                            sc.wait_ge(c_tbu.sem, tbu + kc - 1)
                        sc.copy(tb[kc % 2][:, :], xin[kc % 3][:, :]) \
                            .then_inc(c_ex.sem, 1)
                        sc.square(tb[2 + kc % 2][:, :], xin[kc % 3][:, :]) \
                            .then_inc(c_ex.sem, 1)
                c_ex.n += 2 * NCH

                @block.tensor
                def _(t):
                    for kc in range(NCH):
                        t.wait_ge(c_ex.sem, exb + 2 * (kc + 1))
                        for hf in range(2):
                            sl = slice(512 * hf, 512 * (hf + 1))
                            t.matmul(ps_s1[0:1, sl], ones_bc[:, :],
                                     tb[kc % 2][:, sl], start=(kc == 0),
                                     stop=(kc == NCH - 1))
                            last = t.matmul(ps_s2[0:1, sl], ones_bc[:, :],
                                            tb[2 + kc % 2][:, sl],
                                            start=(kc == 0),
                                            stop=(kc == NCH - 1))
                        last.then_inc(c_tbu.sem, 1)
                c_tbu.n += NCH

                t_bc, t_ps_free = ln_finalize(ps_s1, ps_s2,
                                              (c_tbu.sem, c_tbu.n), D, RO_LN,
                                              bmu[:, :], brstd[:, :], 128)

                xub2 = c_xu.n
                evb = c_ev.n
                sxb2 = slot_bases(sx)

                @block.sync
                def _(s):
                    s.wait_ge(s_lnr.sem, t_bc)
                    for kc in range(NCH):
                        if kc >= 3:
                            s.wait_ge(c_ev.sem, evb + 2 * (kc - 3) + 2)
                        s.dma_start(out=xin[kc % 3][:, :],
                                    in_=src[128 * kc:128 * (kc + 1), :]) \
                            .then_inc(sx[kc % 3].sem, 16)
                slot_bump(sx, 3, NCH)

                @block.vector
                def _(v):
                    v.wait_ge(s_lnr.sem, t_bc)
                    for kc in range(NCH):
                        v.wait_ge(*slot_tok(sx, sxb2, 3, kc))
                        if kc >= 1:
                            v.wait_ge(c_ev.sem, evb + 2 * kc)
                        v.tensor_sub(xo_sb[0][:, :], xin[kc % 3][:, :],
                                     bmu[:, :]).then_inc(c_ev.sem, 1)
                        v.wait_ge(c_ev.sem, evb + 2 * kc + 1)
                        v.tensor_mul(xfb[:, NT * kc:NT * (kc + 1)],
                                     xo_sb[0][:, :], brstd[:, :]) \
                            .then_inc(c_ev.sem, 1)
                c_ev.n += 2 * NCH
                t_xf = c_ev.n

                st1 = contextlib.ExitStack()
                with st1:
                    ps_u = st1.enter_context(
                        nc.psum_tensor(f"pu{tag}", [128, NT], F32))
                    ps_g = st1.enter_context(
                        nc.psum_tensor(f"pg{tag}", [128, 512], F32))

                    @block.tensor
                    def _(t):
                        t.wait_ge(c_ev.sem, t_xf)
                        for kc in range(NCH):
                            for hf in range(2):
                                last = t.matmul(
                                    ps_u[0:GH, 512 * hf:512 * (hf + 1)],
                                    gw1_sb[:, GH * kc:GH * (kc + 1)],
                                    xfb[:, NT * kc + 512 * hf:
                                        NT * kc + 512 * (hf + 1)],
                                    start=(kc == 0), stop=(kc == NCH - 1))
                        last.then_inc(c_mm.sem, 1)
                    c_mm.n += 1
                    t_hmm = c_mm.n

                    @block.vector
                    def _(v):
                        v.wait_ge(c_mm.sem, t_hmm)
                        v.tensor_scalar(h_sb[:, :], ps_u[0:GH, :],
                                        gvec_sb[:, 0:1], None, ALU.add) \
                            .then_inc(c_ev.sem, 1)
                    c_ev.n += 1
                    t_h = c_ev.n

                    @block.scalar
                    def _(sc):
                        sc.wait_ge(c_ev.sem, t_h)
                        sc.copy(hb_sb[:, 0:NT], h_sb[:, :]).then_inc(c_ex.sem, 1)
                        sc.square(hb_sb[:, NT:2 * NT], h_sb[:, :]) \
                            .then_inc(c_ex.sem, 1)
                    c_ex.n += 2
                    t_hb = c_ex.n

                    @block.tensor
                    def _(t):
                        t.wait_ge(c_ex.sem, t_hb)
                        t.wait_ge(c_ev.sem, t_ps_free)
                        for hf in range(2):
                            sl = slice(512 * hf, 512 * (hf + 1))
                            t.matmul(ps_s1[0:1, sl], ones32b[:, :],
                                     hb_sb[:, sl], start=True, stop=True)
                            last = t.matmul(
                                ps_s2[0:1, sl], ones32b[:, :],
                                hb_sb[:, NT + 512 * hf:NT + 512 * (hf + 1)],
                                start=True, stop=True)
                        last.then_inc(c_mm.sem, 1)
                    c_mm.n += 1

                    t_gbc, t_gvar = ln_finalize(ps_s1, ps_s2,
                                                (c_mm.sem, c_mm.n),
                                                GH, RO_G, gbmu[:, :],
                                                gbrs[:, :], GH)

                    @block.vector
                    def _(v):
                        v.wait_ge(s_lnr.sem, t_gbc)
                        v.tensor_sub(h2_sb[:, :], h_sb[:, :], gbmu[:, :]) \
                            .then_inc(c_ev.sem, 1)
                        v.wait_ge(c_ev.sem, c_ev.n + 1)
                        v.tensor_mul(h2_sb[:, :], h2_sb[:, :], gbrs[:, :]) \
                            .then_inc(c_ev.sem, 1)
                    c_ev.n += 2
                    t_hn = c_ev.n

                    @block.scalar
                    def _(sc):
                        sc.wait_ge(c_ev.sem, t_hn)
                        sc.activation(hr_sb[:, :], h2_sb[:, :], AF.Relu,
                                      bias=gvec_sb[:, 2:3],
                                      scale=gvec_sb[:, 1:2]) \
                            .then_inc(c_ex.sem, 1)
                    c_ex.n += 1
                    t_hr = c_ex.n

                    gmm0 = c_mm.n
                    gv0 = c_ev.n

                    @block.tensor
                    def _(t):
                        t.wait_ge(c_ex.sem, t_hr)
                        for tc in range(8):
                            if tc >= 1:
                                t.wait_ge(c_ev.sem, gv0 + 7 * (tc - 1) + 1)
                            t.matmul(ps_g[:, 0:E],
                                     hr_sb[:, 128 * tc:128 * (tc + 1)],
                                     gw2_sb[:, :], start=True,
                                     stop=True).then_inc(c_mm.sem, 1)
                    c_mm.n += 8

                    @block.vector
                    def _(v):
                        for tc in range(8):
                            b = gv0 + 7 * tc
                            v.wait_ge(c_mm.sem, gmm0 + tc + 1)
                            v.tensor_add(gat[:, E * tc:E * (tc + 1)],
                                         ps_g[:, 0:E], gb2B[:, :]) \
                                .then_inc(c_ev.sem, 1)
                            v.wait_ge(c_ev.sem, b + 1)
                            v.reduce_max(mcl[:, 4 * tc:4 * tc + 1],
                                         gat[:, E * tc:E * (tc + 1)], axis=AX) \
                                .then_inc(c_ev.sem, 1)
                            v.wait_ge(c_ev.sem, b + 2)
                            v.tensor_scalar(msk[:, 16 * tc:16 * tc + E],
                                            gat[:, E * tc:E * (tc + 1)],
                                            mcl[:, 4 * tc:4 * tc + 1], None,
                                            ALU.is_equal).then_inc(c_ev.sem, 1)
                            v.wait_ge(c_ev.sem, b + 3)
                            v.scalar_tensor_tensor(
                                gng[:, :], msk[:, 16 * tc:16 * tc + E],
                                NEG_BIG, gat[:, E * tc:E * (tc + 1)],
                                ALU.mult, ALU.add).then_inc(c_ev.sem, 1)
                            v.wait_ge(c_ev.sem, b + 4)
                            v.reduce_max(mcl[:, 4 * tc + 1:4 * tc + 2],
                                         gng[:, :], axis=AX).then_inc(c_ev.sem, 1)
                            v.wait_ge(c_ev.sem, b + 5)
                            v.tensor_scalar(
                                msk[:, 16 * tc + E:16 * tc + 2 * E], gng[:, :],
                                mcl[:, 4 * tc + 1:4 * tc + 2], None,
                                ALU.is_equal).then_inc(c_ev.sem, 1)
                            v.wait_ge(c_ev.sem, b + 6)
                            v.tensor_sub(mcl[:, 4 * tc + 2:4 * tc + 3],
                                         mcl[:, 4 * tc:4 * tc + 1],
                                         mcl[:, 4 * tc + 1:4 * tc + 2]) \
                                .then_inc(c_ev.sem, 1)
                    c_ev.n += 7 * 8
                    t_top2 = c_ev.n

                    sg0 = c_ex.n

                    @block.scalar
                    def _(sc):
                        for tc in range(8):
                            sc.wait_ge(c_ev.sem, gv0 + 7 * tc + 7)
                            sc.activation(mcl[:, 4 * tc + 2:4 * tc + 3],
                                          mcl[:, 4 * tc + 2:4 * tc + 3],
                                          AF.Sigmoid).then_inc(c_ex.sem, 1)
                    c_ex.n += 8

                    @block.tensor
                    def _(t):
                        t.wait_ge(c_ev.sem, t_top2)
                        for tc in range(8):
                            last = t.matmul(ps_g[0:1, 0:16], ones_fc[:, :],
                                            msk[:, 16 * tc:16 * (tc + 1)],
                                            start=(tc == 0), stop=(tc == 7))
                        last.then_inc(c_mm.sem, 1)
                    c_mm.n += 1
                    t_cmm = c_mm.n

                    @block.vector
                    def _(v):
                        v.wait_ge(c_mm.sem, t_cmm)
                        v.tensor_copy(cnt_sb[:, :], ps_g[0:1, 0:16]) \
                            .then_inc(c_ev.sem, 1)
                    c_ev.n += 1
                    t_cnt = c_ev.n

                    rb1 = s_cnt.n

                    @block.sync
                    def _(s):
                        s.wait_ge(c_ev.sem, t_cnt)
                        s.dma_start(out=cin[cidx], in_=cnt_sb[:, :]) \
                            .then_inc(s_cnt.sem, 16)
                    s_cnt.n += 16

                    @block.gpsimd
                    def _(g):
                        g.wait_ge(s_cnt.sem, rb1 + 16)
                        g.collective_compute(
                            "AllReduce", ALU.add,
                            replica_groups=[list(range(8))],
                            ins=[cin[cidx].opt()], outs=[cout[cidx].opt()]) \
                            .then_inc(c_cc.sem, 1)
                    c_cc.n += 1
                    t_ar = c_cc.n

                    @block.sync
                    def _(s):
                        s.wait_ge(c_cc.sem, t_ar)
                        s.dma_start(out=cnt_sb[:, :], in_=cout[cidx]) \
                            .then_inc(s_cnt.sem, 16)
                    s_cnt.n += 16
                    t_cb = s_cnt.n

                    @block.vector
                    def _(v):
                        v.wait_ge(s_cnt.sem, t_cb)
                        v.tensor_scalar(cnt_sb[:, :], cnt_sb[:, :], CAP, ALPHA,
                                        ALU.is_le, ALU.mult) \
                            .then_inc(c_ev.sem, 1)
                    c_ev.n += 1
                    t_alw = c_ev.n

                    rb3 = s_cnt.n

                    @block.sync
                    def _(s):
                        s.wait_ge(c_ev.sem, t_alw)
                        s.dma_start(out=rsc[0:1, RO_ALW:RO_ALW + 16],
                                    in_=cnt_sb[:, :]).then_inc(s_cnt.sem, 16)
                        s.wait_ge(s_cnt.sem, rb3 + 16)
                        s.dma_start(out=alw_sb[:, :],
                                    in_=rsc[0:1, RO_ALW:RO_ALW + 16]
                                    .partition_broadcast(128)) \
                            .then_inc(s_cnt.sem, 16)
                    s_cnt.n += 32
                    t_alb = s_cnt.n

                    wv0 = c_ev.n

                    @block.vector
                    def _(v):
                        v.wait_ge(s_cnt.sem, t_alb)
                        for tc in range(8):
                            b = wv0 + 6 * tc
                            v.wait_ge(c_ex.sem, sg0 + tc + 1)
                            v.tensor_scalar(mcl[:, 4 * tc + 3:4 * tc + 4],
                                            mcl[:, 4 * tc + 2:4 * tc + 3],
                                            -1.0, 1.0, ALU.mult, ALU.add) \
                                .then_inc(c_ev.sem, 1)
                            v.tensor_scalar_mul(gat[:, E * tc:E * (tc + 1)],
                                                msk[:, 16 * tc:16 * tc + E],
                                                mcl[:, 4 * tc + 2:4 * tc + 3]) \
                                .then_inc(c_ev.sem, 1)
                            v.wait_ge(c_ev.sem, b + 2)
                            v.tensor_mul(gat[:, E * tc:E * (tc + 1)],
                                         gat[:, E * tc:E * (tc + 1)],
                                         alw_sb[:, 0:E]).then_inc(c_ev.sem, 1)
                            v.tensor_scalar_mul(
                                gng[:, :], msk[:, 16 * tc + E:16 * tc + 2 * E],
                                mcl[:, 4 * tc + 3:4 * tc + 4]) \
                                .then_inc(c_ev.sem, 1)
                            v.wait_ge(c_ev.sem, b + 4)
                            v.tensor_mul(gng[:, :], gng[:, :],
                                         alw_sb[:, E:2 * E]).then_inc(c_ev.sem, 1)
                            v.wait_ge(c_ev.sem, b + 5)
                            v.tensor_add(w_sb[:, E * tc:E * (tc + 1)],
                                         gat[:, E * tc:E * (tc + 1)],
                                         gng[:, :]).then_inc(c_ev.sem, 1)
                    c_ev.n += 6 * 8

                    wt0 = c_mm.n
                    wte0 = c_ev.n

                    @block.tensor
                    def _(t):
                        for tc in range(8):
                            t.wait_ge(c_ev.sem, wv0 + 6 * (tc + 1))
                            if tc >= 1:
                                t.wait_ge(c_ev.sem, wte0 + tc)
                            t.matmul(ps_g[0:E, 0:128],
                                     w_sb[:, E * tc:E * (tc + 1)],
                                     ident[:, :], start=True, stop=True) \
                                .then_inc(c_mm.sem, 1)
                    c_mm.n += 8

                    @block.vector
                    def _(v):
                        for tc in range(8):
                            v.wait_ge(c_mm.sem, wt0 + tc + 1)
                            v.tensor_copy(wT_sb[:, 128 * tc:128 * (tc + 1)],
                                          ps_g[0:E, 0:128]).then_inc(c_ev.sem, 1)
                    c_ev.n += 8
                    t_wT = c_ev.n

                    @block.tensor
                    def _(t):
                        t.wait_ge(c_ev.sem, t_h)
                        for kc in range(NCH):
                            for hf in range(2):
                                last = t.matmul(
                                    ps_u[:, 512 * hf:512 * (hf + 1)],
                                    lA_sb[:, 128 * kc:128 * (kc + 1)],
                                    xfb[:, NT * kc + 512 * hf:
                                        NT * kc + 512 * (hf + 1)],
                                    start=(kc == 0), stop=(kc == NCH - 1))
                        last.then_inc(c_mm.sem, 1)
                    c_mm.n += 1
                    t_tmm = c_mm.n

                    @block.vector
                    def _(v):
                        v.wait_ge(c_mm.sem, t_tmm)
                        v.tensor_copy(tmp_sb[:, :], ps_u[:, :]) \
                            .then_inc(c_ev.sem, 1)
                    c_ev.n += 1
                    t_tev = c_ev.n

                    we0 = c_mm.n
                    ts0 = c_ev.n

                    @block.tensor
                    def _(t):
                        t.wait_ge(c_ev.sem, t_wT)
                        for hf in range(2):
                            if hf == 1:
                                t.wait_ge(c_ev.sem, ts0 + 1)
                            t.matmul(ps_g[:, :], repl_sb[:, :],
                                     wT_sb[:, 512 * hf:512 * (hf + 1)],
                                     start=True, stop=True).then_inc(c_mm.sem, 1)
                    c_mm.n += 2

                    @block.vector
                    def _(v):
                        v.wait_ge(c_ev.sem, t_tev)
                        for hf in range(2):
                            v.wait_ge(c_mm.sem, we0 + hf + 1)
                            v.tensor_mul(tmps[:, 512 * hf:512 * (hf + 1)],
                                         tmp_sb[:, 512 * hf:512 * (hf + 1)],
                                         ps_g[:, :]).then_inc(c_ev.sem, 1)
                    c_ev.n += 2
                    t_tmps = c_ev.n

                st2 = contextlib.ExitStack()
                with st2:
                    ps_m = [st2.enter_context(
                        nc.psum_tensor(f"pm{i}{tag}", [128, NT], F32))
                        for i in range(2)]
                    wu0 = c_wu.n
                    mm0 = c_mm.n
                    ev0 = c_ev.n
                    ex0 = c_ex.n
                    xub3 = c_xu.n
                    tb0 = c_tbu.n
                    swlb = slot_bases(swl)
                    sxb3 = slot_bases(sx)
                    ssdb = slot_bases(ssd)

                    @block.gpsimd
                    def _(g):
                        for m in range(NCH):
                            if m >= 2:
                                g.wait_ge(c_mm.sem, mm0 + m - 1)
                            g.dma_start(
                                out=wslab[m % 2][:, :].rearrange(
                                    "p (c f) -> p c f", c=NCH),
                                in_=ffnWT[:, 128 * m:128 * (m + 1)].rearrange(
                                    "(c p) f -> p c f", p=128)
                            ).then_inc(swl[m % 2].sem, 16)
                    slot_bump(swl, 2, NCH)

                    def emit_ffn_stat(t, m):
                        t.wait_ge(c_ex.sem, ex0 + 2 * (m + 1))
                        if m == 0:
                            t.wait_ge(c_ev.sem, t_gvar)
                        for hf in range(2):
                            sl = slice(512 * hf, 512 * (hf + 1))
                            t.matmul(ps_s1[0:1, sl], ones_bc[:, :],
                                     tb[m % 2][:, sl], start=(m == 0),
                                     stop=(m == NCH - 1))
                            lasts = t.matmul(ps_s2[0:1, sl],
                                            ones_bc[:, :],
                                            tb[2 + m % 2][:, sl],
                                            start=(m == 0),
                                            stop=(m == NCH - 1))
                        lasts.then_inc(c_tbu.sem, 1)

                    @block.tensor
                    def _(t):
                        t.wait_ge(c_ev.sem, t_tmps)
                        for m in range(NCH):
                            t.wait_ge(*slot_tok(swl, swlb, 2, m))
                            if m >= 2:
                                t.wait_ge(c_ev.sem, ev0 + m - 1)
                            for kc in range(NCH):
                                for hf in range(2):
                                    t.matmul(
                                        ps_m[m % 2][:, 512 * hf:512 * (hf + 1)],
                                        wslab[m % 2][:, 128 * kc:128 * (kc + 1)],
                                        xfb[:, NT * kc + 512 * hf:
                                            NT * kc + 512 * (hf + 1)],
                                        start=(kc == 0), stop=False)
                            for hf in range(2):
                                last = t.matmul(
                                    ps_m[m % 2][:, 512 * hf:512 * (hf + 1)],
                                    lB_sb[:, 128 * m:128 * (m + 1)],
                                    tmps[:, 512 * hf:512 * (hf + 1)],
                                    start=False, stop=True)
                            last.then_inc(c_mm.sem, 1)
                            if next_stats and m >= 2:
                                emit_ffn_stat(t, m - 2)
                        if next_stats:
                            emit_ffn_stat(t, NCH - 2)
                            emit_ffn_stat(t, NCH - 1)
                    c_mm.n += NCH

                    @block.sync
                    def _(s):
                        for m in range(NCH + 2):
                            if m < NCH:
                                if m >= 3:
                                    s.wait_ge(c_ev.sem, ev0 + m - 2)
                                s.dma_start(out=xin[m % 3][:, :],
                                            in_=src[128 * m:128 * (m + 1), :]) \
                                    .then_inc(sx[m % 3].sem, 16)
                            if m >= 2:
                                s.wait_ge(c_ev.sem, ev0 + m - 1)
                                s.dma_start(
                                    out=dst[128 * (m - 2):128 * (m - 1), :],
                                    in_=xo_sb[m % 2][:, :]) \
                                    .then_inc(ssd[m % 2].sem, 16)
                    slot_bump(sx, 3, NCH)
                    slot_bump(ssd, 2, NCH)

                    @block.vector
                    def _(v):
                        for m in range(NCH):
                            v.wait_ge(c_mm.sem, mm0 + m + 1)
                            v.wait_ge(*slot_tok(sx, sxb3, 3, m))
                            if m >= 2:
                                v.wait_ge(*slot_tok(ssd, ssdb, 2, m - 2))
                                if next_stats:
                                    v.wait_ge(c_ex.sem, ex0 + 2 * (m - 1))
                            v.scalar_tensor_tensor(
                                xo_sb[m % 2][:, :], ps_m[m % 2][:, :], 0.5,
                                xin[m % 3][:, :], ALU.mult, ALU.add) \
                                .then_inc(c_ev.sem, 1)
                    c_ev.n += NCH



                    if next_stats:
                        @block.scalar
                        def _(sc):
                            for m in range(NCH):
                                sc.wait_ge(c_ev.sem, ev0 + m + 1)
                                if m >= 2:
                                    sc.wait_ge(c_tbu.sem, tb0 + m - 1)
                                sc.copy(tb[m % 2][:, :], xo_sb[m % 2][:, :]) \
                                    .then_inc(c_ex.sem, 1)
                                sc.square(tb[2 + m % 2][:, :],
                                          xo_sb[m % 2][:, :]) \
                                    .then_inc(c_ex.sem, 1)
                        c_ex.n += 2 * NCH

                        c_tbu.n += NCH
                        t_bc2, _ = ln_finalize(ps_s1, ps_s2,
                                               (c_tbu.sem, c_tbu.n), D,
                                               RO_LN, bmu[:, :],
                                               brstd[:, :], 128)
                    else:
                        t_bc2 = None
                return t_bc2

        t_bc2 = emit_ffn(xT, xres1, 0, True, "A")

        phase_fence()

        phB = contextlib.ExitStack()
        with phB:
            yb = phB.enter_context(nc.sbuf_tensor("ybB", [128, NCH * NT], BF16))
            xin = [phB.enter_context(nc.sbuf_tensor(f"xiB{i}", [128, NT], F32))
                   for i in range(3)]
            xot = phB.enter_context(nc.sbuf_tensor("xtB", [128, NT], F32))
            tbv = [phB.enter_context(nc.sbuf_tensor(f"tvB{i}", [128, NT], BF16))
                   for i in range(2)]
            yo = tbv
            tbq = [phB.enter_context(nc.sbuf_tensor(f"tqB{i}", [128, NT], BF16))
                   for i in range(2)]
            wslab = [phB.enter_context(
                nc.sbuf_tensor(f"wbB{i}", [128, NCH * 128], BF16))
                for i in range(2)]
            qki = [phB.enter_context(nc.sbuf_tensor(f"qiB{i}", [128, NT], BF16))
                   for i in range(2)]
            qko = [phB.enter_context(nc.sbuf_tensor(f"qoB{i}", [128, NT], BF16))
                   for i in range(2)]
            bmu2 = phB.enter_context(nc.sbuf_tensor("bmB", [128, NT], F32))
            brs2 = phB.enter_context(nc.sbuf_tensor("brB", [128, NT], F32))
            srow = phB.enter_context(nc.sbuf_tensor("srB", [1, 2048], F32))
            rwb = phB.enter_context(nc.sbuf_tensor("rwB", [1, 1024], F32))
            ps_q = [phB.enter_context(nc.psum_tensor(f"pBq{i}", [128, NT], F32))
                    for i in range(2)]
            ps_st = [phB.enter_context(nc.psum_tensor(f"pBs{i}", [1, 512], F32))
                     for i in range(4)]

            RAWT = {0: k_raw, 1: q_raw, 2: v_scr}
            APPT = {0: ka_scr, 1: qa_scr}
            WOFF = {0: 2, 1: 0}
            APL = {p: [] for p in range(12)}
            APL["tail"] = []
            for i in range(32):
                APL[6 + i // 8].append((0, i))
            for i in range(32):
                p = 10 + i // 8
                (APL[p] if p <= 11 else APL["tail"]).append((1, i))
            APBYSEQ = []
            for p in list(range(12)) + ["tail"]:
                APBYSEQ.extend(APL[p])
            APSEQ = {key: n for n, key in enumerate(APBYSEQ)}

            mm0 = c_mm.n
            MM = {(p, u): mm0 + 8 * p + u + 1
                  for p in range(12) for u in range(8)}
            c_mm.n += 96
            tb0s = c_tbu.n
            ST = {}
            _n = tb0s
            for p in range(8):
                for u in range(8):
                    _n += 1
                    ST[(p, u)] = _n
            c_tbu.n = _n

            SQ, FACT = {}, {}
            _n = c_ex.n
            for p in range(12):
                if p == 5:
                    FACT[0] = (_n + 1, _n + 2)
                    _n += 2
                if p == 9:
                    FACT[1] = (_n + 1, _n + 2)
                    _n += 2
                if p < 8:
                    for u in range(8):
                        _n += 1
                        SQ[(p, u)] = _n
            c_ex.n = _n

            EVY, EV, CPL, FDVE, APS, APM, APT = {}, {}, {}, {}, {}, {}, {}
            _n = c_ev.n
            for kc in range(NCH):
                EVY[kc] = (_n + 1, _n + 2)
                _n += 2
            for p in range(12):
                na = len(APL[p])
                for u in range(8):
                    _n += 1
                    EV[(p, u)] = _n
                    if u >= 2 and u - 2 < na:
                        key = APL[p][u - 2]
                        APS[key] = _n + 1
                        APM[key] = _n + 2
                        APT[key] = _n + 3
                        _n += 3
                for k_ in range(6, na):
                    key = APL[p][k_]
                    APS[key] = _n + 1
                    APM[key] = _n + 2
                    APT[key] = _n + 3
                    _n += 3
                if p < 8:
                    _n += 4
                    CPL[p] = _n
                if p == 4:
                    FDVE[0] = (_n + 1, _n + 12)
                    _n += 12
                if p == 9:
                    FDVE[1] = (_n + 1, _n + 12)
                    _n += 12
            for (tn, i) in APL["tail"]:
                APS[(tn, i)] = _n + 1
                APM[(tn, i)] = _n + 2
                APT[(tn, i)] = _n + 3
                _n += 3
            c_ev.n = _n

            sxb = slot_bases(sx)
            syab = slot_bases(sya)
            sybb = slot_bases(syb)
            swlb = slot_bases(swl)
            sstb = slot_bases(sst)
            stbb = slot_bases(stb)
            sqab = slot_bases(sqa)
            sarb = s_ar.n

            def SARSND(p):
                return sarb + 32 * (p + 1) + (16 if p >= 4 else 0)
            SAR_RK = sarb + 32 * 5 + 16
            SAR_RQ = sarb + 32 * 8 + 32

            def YST(kc):
                return sya[kc % 2].sem, syab[kc % 2] + 16 * (kc // 2 + 1)

            YGRP = []
            for a, n_ in enumerate([4, 4, 8, 8, 8]):
                for r in range(n_):
                    YGRP.append((a, r))
            YGO = [0, 4, 8, 16, 24]

            def YBL(p, kc):
                a = YGRP[kc][0]
                return syb[a].sem, sybb[a] + 16 * (p + 1)

            def SWL(ui):
                return swl[ui % 2].sem, swlb[ui % 2] + 16 * (ui // 2 + 1)

            def SSTK(ui):
                return sst[ui % 2].sem, sstb[ui % 2] + 16 * (ui // 2 + 1)

            def QKL(ki):
                return stb[ki % 2].sem, stbb[ki % 2] + 16 * (ki // 2 + 1)

            def QKS(ki):
                return sqa[ki % 2].sem, sqab[ki % 2] + 16 * (ki // 2 + 1)

            ybr0 = c_ybr.n
            c_ybr.n += 31 * 12
            cc0 = c_cc.n

            def T_AGY(a):
                return cc0 + a + 1
            T_ARK, T_ARQ = cc0 + 6, cc0 + 7
            c_cc.n += 7

            BNC = {}
            _b = s_bnc.n
            for tn in range(2):
                for s_ in range(4):
                    BNC[(tn, s_)] = _b + 64 * (4 * tn + s_ + 1)
            s_bnc.n = _b + 64 * 8

            def emit_apply(v, tn, i):
                s_, r_ = divmod(i, 8)
                ki = APSEQ[(tn, i)]
                v.wait_ge(*QKL(ki))
                v.wait_ge(s_bnc.sem, BNC[(tn, s_)])
                if ki >= 1:
                    v.wait_ge(c_ev.sem, APT[APBYSEQ[ki - 1]])
                else:
                    v.wait_ge(c_ev.sem, EVY[NCH - 1][1])
                v.tensor_sub(xot[:, :], qki[ki % 2][:, :],
                             bmu2[:, :]).then_inc(c_ev.sem, 1)
                v.wait_ge(c_ev.sem, APS[(tn, i)])
                v.tensor_mul(xot[:, :], xot[:, :],
                             brs2[:, :]).then_inc(c_ev.sem, 1)
                v.wait_ge(c_ev.sem, APM[(tn, i)])
                if ki >= 2:
                    v.wait_ge(*QKS(ki - 2))
                woff = WOFF[tn]
                v.tensor_scalar(qko[ki % 2][:, :], xot[:, :],
                                qkln_sb[:, 4 * r_ + woff:4 * r_ + woff + 1],
                                qkln_sb[:, 4 * r_ + woff + 1:
                                        4 * r_ + woff + 2],
                                ALU.mult, ALU.add).then_inc(c_ev.sem, 1)

            def emit_finalize_dve(v, tn):
                v.wait_ge(s_ar.sem, SAR_RK if tn == 0 else SAR_RQ)
                base = FDVE[tn][0] - 1
                for s_ in range(4):
                    mu = rows[0:1, 1024 * s_:1024 * (s_ + 1)]
                    s2 = rows[0:1, 4096 + 1024 * s_:4096 + 1024 * (s_ + 1)]
                    if s_ >= 1:
                        v.wait_ge(c_ev.sem, base + 3 * s_)
                    v.tensor_scalar_mul(mu, mu, 1.0 / D).then_inc(c_ev.sem, 1)
                    v.wait_ge(c_ev.sem, base + 3 * s_ + 1)
                    v.tensor_mul(rwb[0:1, :], mu, mu).then_inc(c_ev.sem, 1)
                    v.wait_ge(c_ev.sem, base + 3 * s_ + 2)
                    v.scalar_tensor_tensor(s2, s2, 1.0 / D, rwb[0:1, :],
                                           ALU.mult, ALU.subtract) \
                        .then_inc(c_ev.sem, 1)

            @block.vector
            def _(v):
                v.wait_ge(s_lnr.sem, t_bc2)
                for kc in range(NCH):
                    v.wait_ge(*slot_tok(sx, sxb, 3, kc))
                    if kc >= 1:
                        v.wait_ge(c_ev.sem, EVY[kc - 1][1])
                    v.tensor_sub(xot[:, :], xin[kc % 3][:, :],
                                 bmu[:, :]).then_inc(c_ev.sem, 1)
                    if kc >= 2:
                        v.wait_ge(*YST(kc - 2))
                    v.wait_ge(c_ev.sem, EVY[kc][0])
                    v.tensor_mul(yo[kc % 2][:, :], xot[:, :],
                                 brstd[:, :]).then_inc(c_ev.sem, 1)

            for p in range(12):
                @block.vector
                def _(v, p=p):
                    na = len(APL[p])
                    for u in range(8):
                        ui = 8 * p + u
                        v.wait_ge(c_mm.sem, MM[(p, u)])
                        if ui < 2:
                            v.wait_ge(*YST(NCH - 2 + ui))
                        if ui >= 2:
                            v.wait_ge(*SSTK(ui - 2))
                            pp, uu = divmod(ui - 2, 8)
                            if (pp, uu) in SQ:
                                v.wait_ge(c_ex.sem, SQ[(pp, uu)])
                            if (pp, uu) in ST:
                                v.wait_ge(c_tbu.sem, ST[(pp, uu)])
                        v.tensor_scalar(tbv[ui % 2][:, :], ps_q[ui % 2][:, :],
                                        8.0, -8.0, ALU.min, ALU.max) \
                            .then_inc(c_ev.sem, 1)
                        if u >= 2 and u - 2 < na:
                            tn, i = APL[p][u - 2]
                            emit_apply(v, tn, i)
                    for k_ in range(6, na):
                        tn, i = APL[p][k_]
                        emit_apply(v, tn, i)
                    if p < 8:
                        v.wait_ge(c_tbu.sem, ST[(p, 7)])
                        if p >= 1:
                            v.wait_ge(s_ar.sem, SARSND(p - 1))
                        for q4 in range(4):
                            dst = 512 * (q4 % 2) + 1024 * (q4 // 2)
                            v.tensor_copy(srow[0:1, dst:dst + 512],
                                          ps_st[q4][0:1, :]) \
                                .then_inc(c_ev.sem, 1)
                    if p == 4:
                        emit_finalize_dve(v, 0)
                    if p == 9:
                        emit_finalize_dve(v, 1)

            @block.vector
            def _(v):
                for (tn, i) in APL["tail"]:
                    emit_apply(v, tn, i)

            @block.scalar
            def _(sc):
                for p in range(12):
                    if p in (5, 9):
                        tn = 0 if p == 5 else 1
                        sc.wait_ge(c_ev.sem, FDVE[tn][1])
                        sc.activation(rows[0:1, 4096:8192],
                                      rows[0:1, 4096:8192], AF.Ln,
                                      bias=epsc[0:1, 0:1]).then_inc(c_ex.sem, 1)
                        sc.wait_ge(c_ex.sem, FACT[tn][0])
                        sc.activation(rows[0:1, 4096:8192],
                                      rows[0:1, 4096:8192], AF.Exp,
                                      scale=-0.5).then_inc(c_ex.sem, 1)
                    if p < 8:
                        for u in range(8):
                            ui = 8 * p + u
                            sc.wait_ge(c_ev.sem, EV[(p, u)])
                            if ui >= 2:
                                pp, uu = divmod(ui - 2, 8)
                                if (pp, uu) in ST:
                                    sc.wait_ge(c_tbu.sem, ST[(pp, uu)])
                            sc.square(tbq[ui % 2][:, :], tbv[ui % 2][:, :]) \
                                .then_inc(c_ex.sem, 1)

            def emit_stat(t, p, u):
                t.wait_ge(c_ex.sem, SQ[(p, u)])
                if u == 0 and p >= 1:
                    t.wait_ge(c_ev.sem, CPL[p - 1])
                ui = 8 * p + u
                for q4 in range(4):
                    src = tbv if q4 < 2 else tbq
                    last = t.matmul(
                        ps_st[q4][0:1, :], ones_bc[:, :],
                        src[ui % 2][:, 512 * (q4 % 2):512 * (q4 % 2 + 1)],
                        start=(u == 0), stop=(u == 7))
                last.then_inc(c_tbu.sem, 1)

            @block.tensor
            def _(t):
                for p in range(12):
                    third = p // 4
                    for u in range(8):
                        ui = 8 * p + u
                        m = 8 * third + u
                        t.wait_ge(*SWL(ui))
                        if ui >= 2:
                            t.wait_ge(c_ev.sem, EV[divmod(ui - 2, 8)])
                        for kc in range(NCH):
                            if u == 0 and kc in YGO:
                                t.wait_ge(*YBL(p, kc))
                            for hf in range(2):
                                last = t.matmul(
                                    ps_q[ui % 2][:, 512 * hf:512 * (hf + 1)],
                                    wslab[ui % 2][:, 128 * kc:128 * (kc + 1)],
                                    yb[:, NT * kc + 512 * hf:
                                       NT * kc + 512 * (hf + 1)],
                                    start=(kc == 0), stop=(kc == NCH - 1))
                            if u == 7 and kc < NCH - 1:
                                last.then_inc(c_ybr.sem, 1)
                        last.then_inc(c_mm.sem, 1)
                        if p < 8 and u >= 2:
                            emit_stat(t, p, u - 2)
                    if p < 8:
                        emit_stat(t, p, 6)
                        emit_stat(t, p, 7)

            @block.sync
            def _(s):
                for kc in range(NCH + 2):
                    if kc < NCH:
                        if kc >= 3:
                            s.wait_ge(c_ev.sem, EVY[kc - 3][0])
                        s.dma_start(out=xin[kc % 3][:, :],
                                    in_=xres1[128 * kc:128 * (kc + 1), :]) \
                            .then_inc(sx[kc % 3].sem, 16)
                    if kc >= 2:
                        kp = kc - 2
                        ga, gr = YGRP[kp]
                        s.wait_ge(c_ev.sem, EVY[kp][1])
                        s.dma_start(
                            out=agy_i[ga][128 * gr:128 * (gr + 1), :],
                            in_=yo[kp % 2][:, :]).then_inc(sya[kp % 2].sem, 16)
            slot_bump(sx, 3, NCH)
            slot_bump(sya, 2, NCH)

            for p in range(12):
                @block.sync
                def _(s, p=p):
                    third, tb = p // 4, p % 4
                    _yg = [4, 4, 8, 8, 8]
                    for a in range(5):
                        n_ = _yg[a]
                        lastc = YGO[a] + n_ - 1
                        if p >= 1:
                            if lastc < NCH - 1:
                                s.wait_ge(c_ybr.sem,
                                          ybr0 + 31 * (p - 1) + lastc + 1)
                            else:
                                s.wait_ge(c_mm.sem, MM[(p - 1, 7)])
                        s.wait_ge(c_cc.sem, T_AGY(a))
                        s.dma_start(
                            out=yb[:, NT * YGO[a]:NT * (YGO[a] + n_)]
                            .rearrange("p (r t) -> p r t", r=n_),
                            in_=agy_o[a][128 * n_ * tb:128 * n_ * (tb + 1), :]
                            .rearrange("(r p) t -> p r t", p=128)) \
                            .then_inc(syb[a].sem, 16)
                    for u in range(8):
                        ui = 8 * p + u
                        s.wait_ge(c_ev.sem, EV[(p, u)])
                        if p < 8:
                            s.wait_ge(c_ex.sem, SQ[(p, u)])
                        s.dma_start(
                            out=RAWT[third][128 * u:128 * (u + 1),
                                            1024 * tb:1024 * (tb + 1)],
                            in_=tbv[ui % 2][:, :]).then_inc(sst[ui % 2].sem, 16)
                    if p < 8:
                        tn, tbs = p // 4, p % 4
                        dst = arki if tn == 0 else arqi
                        s.wait_ge(c_ev.sem, CPL[p])
                        s.dma_start(out=dst[0:1, 1024 * tbs:1024 * (tbs + 1)],
                                    in_=srow[0:1, 0:1024]) \
                            .then_inc(s_ar.sem, 16)
                        s.dma_start(out=dst[0:1, 4096 + 1024 * tbs:
                                            4096 + 1024 * (tbs + 1)],
                                    in_=srow[0:1, 1024:2048]) \
                            .then_inc(s_ar.sem, 16)
                    if p in (4, 9):
                        tn = 0 if p == 4 else 1
                        s.wait_ge(c_cc.sem, T_ARK if tn == 0 else T_ARQ)
                        if tn == 0:
                            s.wait_ge(s_lnr.sem, t_bc2)
                        else:
                            s.wait_ge(s_bnc.sem, BNC[(0, 3)])
                        s.dma_start(out=rows[0:1, :],
                                    in_=(arko if tn == 0 else arqo)) \
                            .then_inc(s_ar.sem, 16)
            for _p in range(12):
                for _a in range(5):
                    syb[_a].n += 16
            slot_bump(sst, 2, 96)
            slot_bump(stb, 2, 64)
            slot_bump(sqa, 2, 64)
            s_ar.n += 16 * 18

            def emit_apply_load(g, tn, i):
                s_i, r_ = divmod(i, 8)
                ki = APSEQ[(tn, i)]
                if ki >= 2:
                    g.wait_ge(c_ev.sem, APS[APBYSEQ[ki - 2]])
                rawp = 4 * tn + s_i
                g.wait_ge(*SSTK(8 * rawp + r_))
                g.dma_start(out=qki[ki % 2][:, :],
                            in_=RAWT[tn][128 * r_:128 * (r_ + 1),
                                         1024 * s_i:1024 * (s_i + 1)]) \
                    .then_inc(stb[ki % 2].sem, 16)

            def emit_apply_store(g, tn, i):
                s_i, r_ = divmod(i, 8)
                ki = APSEQ[(tn, i)]
                g.wait_ge(c_ev.sem, APT[(tn, i)])
                g.dma_start(out=APPT[tn][128 * r_:128 * (r_ + 1),
                                         1024 * s_i:1024 * (s_i + 1)],
                            in_=qko[ki % 2][:, :]).then_inc(sqa[ki % 2].sem, 16)

            def emit_bounce(g, tn, s_):
                base = BNC[(tn, s_)] - 64
                g.wait_ge(s_bnc.sem, base)
                if tn == 0 and s_ == 0:
                    g.wait_ge(s_lnr.sem, t_bc2)
                g.wait_ge(c_ev.sem, FDVE[tn][0])
                g.wait_ge(c_ex.sem, FACT[tn][1])
                idx = 4 * tn + s_
                if idx >= 1:
                    tp, sp = divmod(idx - 1, 4)
                    g.wait_ge(c_ev.sem, APT[(tp, 8 * sp + 7)])
                g.dma_start(out=rsc[0:1, RO_LN:RO_LN + NT],
                            in_=rows[0:1, 1024 * s_:1024 * (s_ + 1)]) \
                    .then_inc(s_bnc.sem, 16)
                g.dma_start(out=rsc[0:1, RO_LN + NT:RO_LN + 2 * NT],
                            in_=rows[0:1, 4096 + 1024 * s_:
                                     4096 + 1024 * (s_ + 1)]) \
                    .then_inc(s_bnc.sem, 16)
                g.wait_ge(s_bnc.sem, base + 32)
                g.dma_start(out=bmu2[:, :],
                            in_=rsc[0:1, RO_LN:RO_LN + NT]
                            .partition_broadcast(128)).then_inc(s_bnc.sem, 16)
                g.dma_start(out=brs2[:, :],
                            in_=rsc[0:1, RO_LN + NT:RO_LN + 2 * NT]
                            .partition_broadcast(128)).then_inc(s_bnc.sem, 16)

            @block.gpsimd
            def _(g):
                _cum = [4, 8, 16, 24, 32]
                for a in range(5):
                    need = 16 * (_cum[a] // 2)
                    g.wait_ge(sya[0].sem, syab[0] + need)
                    g.wait_ge(sya[1].sem, syab[1] + need)
                    g.collective_compute(
                        "AllGather", ALU.bypass,
                        replica_groups=[[0, 1, 2, 3], [4, 5, 6, 7]],
                        ins=[agy_i[a].opt()], outs=[agy_o[a].opt()]) \
                        .then_inc(c_cc.sem, 1)
                for p in range(12):
                    third = p // 4
                    na = len(APL[p])
                    for u in range(8):
                        ui = 8 * p + u
                        m = 8 * third + u
                        if ui >= 2:
                            g.wait_ge(c_mm.sem, MM[divmod(ui - 2, 8)])
                        g.dma_start(
                            out=wslab[ui % 2][:, :].rearrange(
                                "p (c f) -> p c f", c=NCH),
                            in_=wqkvT[:, 128 * m:128 * (m + 1)].rearrange(
                                "(c p) f -> p c f", p=128)
                        ).then_inc(swl[ui % 2].sem, 16)
                        if u >= 2 and u - 2 < na:
                            tn, i = APL[p][u - 2]
                            emit_apply_load(g, tn, i)
                            if u - 4 >= 0 and u - 4 < na:
                                emit_apply_store(g, *APL[p][u - 4])
                    for k_ in range(6, na):
                        tn, i = APL[p][k_]
                        emit_apply_load(g, tn, i)
                        emit_apply_store(g, *APL[p][k_ - 2])
                    if p in (3, 7):
                        tn = 0 if p == 3 else 1
                        g.wait_ge(s_ar.sem, SARSND(p))
                        g.collective_compute(
                            "AllReduce", ALU.add,
                            replica_groups=[[0, 1, 2, 3], [4, 5, 6, 7]],
                            ins=[(arki if tn == 0 else arqi).opt()],
                            outs=[(arko if tn == 0 else arqo).opt()]) \
                            .then_inc(c_cc.sem, 1)
                    if 5 <= p <= 8:
                        emit_bounce(g, 0, p - 5)
                    if 9 <= p <= 11:
                        emit_bounce(g, 1, p - 9)
                    for k_ in range(max(0, na - 2), na):
                        emit_apply_store(g, *APL[p][k_])

            @block.gpsimd
            def _(g):
                for (tn, i) in APL["tail"][:8]:
                    emit_apply_load(g, tn, i)
                    emit_apply_store(g, tn, i)
                emit_bounce(g, 1, 3)
                for (tn, i) in APL["tail"][8:]:
                    emit_apply_load(g, tn, i)
                    emit_apply_store(g, tn, i)
            slot_bump(swl, 2, 96)

        phase_fence()

        phC = contextlib.ExitStack()
        with phC:
            kh = [phC.enter_context(nc.sbuf_tensor(f"kh{i}", [128, S], BF16))
                  for i in range(2)]
            qh = [phC.enter_context(nc.sbuf_tensor(f"qh{i}", [128, S], BF16))
                  for i in range(2)]
            vth = phC.enter_context(nc.sbuf_tensor("vth", [128, S], BF16))
            vh = [phC.enter_context(nc.sbuf_tensor(f"vh{i}", [128, S], BF16))
                  for i in range(2)]
            nsq = phC.enter_context(nc.sbuf_tensor("nsq", [128, S], BF16))
            crb = phC.enter_context(nc.sbuf_tensor("crb", [1, S], BF16))
            aq_sb = phC.enter_context(nc.sbuf_tensor("aq_sb", [1, S], F32))
            ptl = [phC.enter_context(nc.sbuf_tensor(f"ptl{i}", [128, 512], BF16))
                   for i in range(4)]
            dbb = [phC.enter_context(nc.sbuf_tensor(f"dbb{i}", [128, 512], F32))
                   for i in range(2)]
            obf = [phC.enter_context(nc.sbuf_tensor(f"obf{i}", [128, 512], BF16))
                   for i in range(2)]
            sca = phC.enter_context(nc.sbuf_tensor("sca", [1, 8], F32))
            ps_sc = [phC.enter_context(nc.psum_tensor(f"pCs{i}", [128, 512], F32))
                     for i in range(3)]
            ps_o = [phC.enter_context(nc.psum_tensor(f"pCo{i}", [128, 512], F32))
                    for i in range(2)]
            ps_vt = phC.enter_context(nc.psum_tensor("pCv", [128, 512], BF16))
            ps_dn = [phC.enter_context(nc.psum_tensor(f"pCd{i}", [1, 512], F32))
                     for i in range(2)]
            ps_nm = ps_dn[0]

            tiles = []
            for hh in range(HPC):
                for qb in range(8):
                    for kc in range(4 * qb + 4):
                        tiles.append((hh, qb, kc))
            NTL = len(tiles)
            last_ti_of_head = {}
            last_ti_of_g = {}
            for ti, (hh, qb, kc) in enumerate(tiles):
                last_ti_of_head[hh] = ti
                last_ti_of_g[8 * hh + qb] = ti

            def qs_of(qb, kc):
                return max(0, 128 * kc - 512 * qb)

            cmm0, cex0, cev0 = c_mm.n, c_ev.n, c_ex.n
            cev0, cex0 = c_ev.n, c_ex.n
            amm0, asu0, aex0 = c_amm.n, c_asu.n, c_aex.n
            apv0, aac0, aon0, adn0 = c_apv.n, c_aac.n, c_aon.n, c_adn.n
            skhb, sqhb = slot_bases(skh), slot_bases(sqh)
            svtb = svt.n
            sobb, sdvb, sdbb = slot_bases(sob), slot_bases(sdv), slot_bases(sdb)
            aqb, cpbb, cbbb = s_aq.n, s_cpb.n, s_cbb.n

            def KH_TOK(h):
                return skh[h % 2].sem, skhb[h % 2] + 16 * (h // 2 + 1)

            def QH_TOK(h):
                return sqh[h % 2].sem, sqhb[h % 2] + 16 * (h // 2 + 1)

            def VT_TOK(h):
                return svt.sem, svtb + 16 * (h + 1)

            def SOB_TOK(g):
                return sob[g % 2].sem, sobb[g % 2] + 16 * (g // 2 + 1)

            def SDV_TOK(g):
                return sdv[g % 2].sem, sdvb[g % 2] + 16 * (g // 2 + 1)

            def SDB_TOK(g):
                return sdb[g % 2].sem, sdbb[g % 2] + 16 * (g // 2 + 1)

            def TK(h, b):
                return cmm0 + 24 * h + b + 1

            def TQ(h, b):
                return cmm0 + 24 * h + 8 + b + 1

            def TV(h, g):
                return cmm0 + 24 * h + 16 + g + 1

            def SQK(h):
                return cex0 + 12 * h + 1

            def SQQ(h):
                return cex0 + 12 * h + 2

            def SQM(h):
                return cex0 + 12 * h + 3

            def SQN(h):
                return cex0 + 12 * h + 4

            def VEV(h, g):
                return cex0 + 12 * h + 5 + g

            def KEV(h, b):
                return cev0 + 36 * h + b + 1

            def KMAX(h):
                return cev0 + 36 * h + 9

            def QEV(h, b):
                return cev0 + 36 * h + 10 + b

            def KSCL(h):
                return cev0 + 36 * h + 18

            def CPP(h):
                return cev0 + 36 * h + 19

            def CRB(h):
                return cev0 + 36 * h + 20

            def CLAMP(h, qb):
                return cev0 + 36 * h + 20 + 2 * qb + 1

            def RECIP(h, qb):
                return cev0 + 36 * h + 20 + 2 * qb + 2

            def AQL(h):
                return aqb + 16 * (h + 1)

            def CPB(h):
                return cpbb + 16 * (h + 1)

            def CBB(h):
                return cbbb + 16 * (h + 1)


            def emit_kq_loads(g, h):
                if h >= 2:
                    g.wait_ge(c_amm.sem, amm0 + last_ti_of_head[h - 2] + 1)
                g.dma_start(out=kh[h % 2][:, :],
                            in_=ka_scr[128 * h:128 * (h + 1), :]) \
                    .then_inc(skh[h % 2].sem, 16)
                g.dma_start(out=qh[h % 2][:, :],
                            in_=qa_scr[128 * h:128 * (h + 1), :]) \
                    .then_inc(sqh[h % 2].sem, 16)

            def emit_v_loads(g, h):
                if h >= 1:
                    g.wait_ge(c_mm.sem, TV(h - 1, 7))
                g.dma_start(out=vth[:, :],
                            in_=v_scr[128 * h:128 * (h + 1), :]) \
                    .then_inc(svt.sem, 16)

            @block.gpsimd
            def _(g):
                emit_kq_loads(g, 0)
                emit_v_loads(g, 0)
                for h in range(1, HPC):
                    emit_kq_loads(g, h)
                    emit_v_loads(g, h)
            for h in range(HPC):
                skh[h % 2].n += 16
                sqh[h % 2].n += 16
            svt.n += 16 * HPC

            @block.gpsimd
            def _(g):
                g.wait_ge(sob[0].sem, sobb[0] + 16 * 28)
                g.wait_ge(sob[1].sem, sobb[1] + 16 * 28)
                g.collective_compute(
                    "AllGather", ALU.bypass,
                    replica_groups=[[0, 1, 2, 3], [4, 5, 6, 7]],
                    ins=[aoiA.opt()], outs=[aooA.opt()]).then_inc(c_cc.sem, 1)
                g.wait_ge(sob[0].sem, sobb[0] + 16 * 32)
                g.wait_ge(sob[1].sem, sobb[1] + 16 * 32)
                g.collective_compute(
                    "AllGather", ALU.bypass,
                    replica_groups=[[0, 1, 2, 3], [4, 5, 6, 7]],
                    ins=[aoiB.opt()], outs=[aooB.opt()]).then_inc(c_cc.sem, 1)
            c_cc.n += 2
            T_AGO1, T_AGO2 = c_cc.n - 1, c_cc.n

            def emit_pv(t, tj):
                hj, qj, kj = tiles[tj]
                gj = 8 * hj + qj
                qsj = qs_of(qj, kj)
                t.wait_ge(c_aex.sem, aex0 + tj + 1)
                t.wait_ge(c_ex.sem, VEV(hj, kj // 4))
                if kj == 0 and gj >= 2:
                    t.wait_ge(c_aon.sem, aon0 + gj - 1)
                t.matmul(ps_o[gj % 2][:, qsj:512],
                         vh[hj % 2][:, 128 * kj:128 * (kj + 1)],
                         ptl[tj % 4][:, qsj:512],
                         start=(kj == 0), stop=(kj == 4 * qj + 3)) \
                    .then_inc(c_apv.sem, 1)
                if kj == 0:
                    if gj >= 2:
                        h2, q2 = divmod(gj - 2, 8)
                        t.wait_ge(c_ev.sem, CLAMP(h2, q2))
                    if qj == 0:
                        t.wait_ge(c_ev.sem, QEV(hj, 7))
                t.matmul(ps_dn[qj % 2][0:1, qsj:512], ones_bc[:, :],
                         ptl[tj % 4][:, qsj:512],
                         start=(kj == 0), stop=(kj == 4 * qj + 3)) \
                    .then_inc(c_adn.sem, 1)

            @block.tensor
            def _(t):
                pend_pv = []
                for ti, (hh, qb, kc) in enumerate(tiles):
                    g = 8 * hh + qb
                    if qb == 0 and kc == 0:
                        while pend_pv:
                            emit_pv(t, pend_pv.pop(0))
                    if qb == 0 and kc == 0:
                        t.wait_ge(c_ex.sem, SQK(hh))
                        for b in range(8):
                            if b == 0 and hh >= 1:
                                t.wait_ge(c_ev.sem, QEV(hh - 1, 7))
                                t.wait_ge(c_ev.sem, CLAMP(hh - 1, 6))
                            if b >= 1:
                                t.wait_ge(c_ev.sem, KEV(hh, b - 1))
                            t.matmul(ps_nm[0:1, :], ones_bc[:, :],
                                     nsq[:, 512 * b:512 * (b + 1)],
                                     start=True, stop=True).then_inc(c_mm.sem, 1)
                        t.wait_ge(c_ex.sem, SQQ(hh))
                        for b in range(8):
                            t.wait_ge(c_ev.sem,
                                      KEV(hh, 7) if b == 0 else QEV(hh, b - 1))
                            t.matmul(ps_nm[0:1, :], ones_bc[:, :],
                                     nsq[:, 512 * b:512 * (b + 1)],
                                     start=True, stop=True).then_inc(c_mm.sem, 1)
                        t.wait_ge(*VT_TOK(hh))
                        for grp in range(8):
                            t.wait_ge(c_ex.sem,
                                      VEV(hh, grp - 1) if grp else
                                      (VEV(hh - 1, 7) if hh else 0))
                            for i4 in range(4):
                                kk = 4 * grp + i4
                                last = t.transpose(
                                    ps_vt[:, 128 * i4:128 * (i4 + 1)],
                                    vth[:, 128 * kk:128 * (kk + 1)],
                                    ident[:, :])
                            last.then_inc(c_mm.sem, 1)
                    if qb == 0 and kc == 0:
                        t.wait_ge(*KH_TOK(hh))
                        t.wait_ge(*QH_TOK(hh))
                        t.wait_ge(c_ev.sem, CRB(hh))
                    if ti >= 3:
                        t.wait_ge(c_aex.sem, aex0 + ti - 2)
                    qs = qs_of(qb, kc)
                    t.matmul(ps_sc[ti % 3][:, qs:512],
                             kh[hh % 2][:, 128 * kc:128 * (kc + 1)],
                             qh[hh % 2][:, 512 * qb + qs:512 * (qb + 1)],
                             start=True, stop=False)
                    t.matmul(ps_sc[ti % 3][:, qs:512], ones_r[0:1, :],
                             crb[0:1, 512 * qb + qs:512 * (qb + 1)],
                             start=False, stop=True).then_inc(c_amm.sem, 1)
                    pend_pv.append(ti)
                    if len(pend_pv) > 2:
                        emit_pv(t, pend_pv.pop(0))
                for tj in pend_pv:
                    emit_pv(t, tj)
            c_mm.n += 24 * HPC
            c_amm.n += NTL
            c_apv.n += NTL
            c_adn.n += NTL

            msk0 = c_msk.n
            mcount = []
            _mc = 0
            for (hh, qb, kc) in tiles:
                if kc >= 4 * qb:
                    _mc += 1
                mcount.append(_mc)

            @block.scalar
            def _(sc):
                for ti, (hh, qb, kc) in enumerate(tiles):
                    if qb == 0 and kc == 0:
                        sc.wait_ge(*KH_TOK(hh))
                        if hh >= 1:
                            sc.wait_ge(c_mm.sem, TQ(hh - 1, 7))
                        sc.square(nsq[:, :], kh[hh % 2][:, :]) \
                            .then_inc(c_ex.sem, 1)
                        sc.wait_ge(c_mm.sem, TK(hh, 7))
                        sc.wait_ge(*QH_TOK(hh))
                        sc.square(nsq[:, :], qh[hh % 2][:, :]) \
                            .then_inc(c_ex.sem, 1)
                        sc.wait_ge(c_ev.sem, KMAX(hh))
                        sc.activation(sca[0:1, 1:2], sca[0:1, 0:1],
                                      AF.Sqrt).then_inc(c_ex.sem, 1)
                        sc.wait_ge(c_ev.sem, QEV(hh, 7))
                        sc.activation(rows[0:1, 4096:8192],
                                      rows[0:1, 4096:8192], AF.Sqrt) \
                            .then_inc(c_ex.sem, 1)
                        for grp in range(8):
                            sc.wait_ge(c_mm.sem, TV(hh, grp))
                            if grp == 0 and hh >= 2:
                                sc.wait_ge(
                                    c_apv.sem,
                                    apv0 + last_ti_of_head[hh - 2] + 1)
                            sc.copy(vh[hh % 2][:, 512 * grp:512 * (grp + 1)],
                                    ps_vt[:, :]).then_inc(c_ex.sem, 1)
                    qs = qs_of(qb, kc)
                    if kc >= 4 * qb:
                        sc.wait_ge(c_msk.sem, msk0 + mcount[ti])
                    else:
                        sc.wait_ge(c_amm.sem, amm0 + ti + 1)
                    if ti >= 4:
                        sc.wait_ge(c_apv.sem, apv0 + ti - 3)
                    sc.activation(ptl[ti % 4][:, qs:512],
                                  ps_sc[ti % 3][:, qs:512], AF.Exp,
                                  bias=alibik_sb[:, 32 * hh + kc:
                                                 32 * hh + kc + 1],
                                  scale=SCALE).then_inc(c_aex.sem, 1)
            c_ex.n += 12 * HPC
            c_aex.n += NTL

            @block.vector
            def _(v):
                pend_on = []
                for ti, (hh, qb, kc) in enumerate(tiles):
                    g = 8 * hh + qb
                    if qb == 0 and kc == 0:
                        for b in range(8):
                            v.wait_ge(c_mm.sem, TK(hh, b))
                            if b == 0 and hh >= 1:
                                v.wait_ge(*SDV_TOK(8 * (hh - 1) + 7))
                                v.wait_ge(*SDV_TOK(8 * (hh - 1) + 6))
                                v.wait_ge(c_ex.sem, SQN(hh - 1))
                            if b == 0 and hh == 0:
                                v.wait_ge(s_lnr.sem, s_lnr.n)
                            v.tensor_copy(rows[0:1, 512 * b:512 * (b + 1)],
                                          ps_nm[0:1, :]).then_inc(c_ev.sem, 1)
                        v.wait_ge(c_ev.sem, KEV(hh, 7))
                        v.reduce_max(sca[0:1, 0:1], rows[0:1, 0:4096],
                                     axis=AX).then_inc(c_ev.sem, 1)
                        for b in range(8):
                            v.wait_ge(c_mm.sem, TQ(hh, b))
                            v.tensor_copy(
                                rows[0:1, 4096 + 512 * b:4096 + 512 * (b + 1)],
                                ps_nm[0:1, :]).then_inc(c_ev.sem, 1)
                        v.wait_ge(c_ex.sem, SQM(hh))
                        v.tensor_scalar_mul(sca[0:1, 2:3], sca[0:1, 1:2],
                                            1.01).then_inc(c_ev.sem, 1)
                        v.wait_ge(c_ex.sem, SQN(hh))
                        v.wait_ge(c_ev.sem, KSCL(hh))
                        v.wait_ge(s_aq.sem, AQL(hh))
                        v.scalar_tensor_tensor(
                            rows[0:1, 0:4096], rows[0:1, 4096:8192],
                            sca[0:1, 2:3], aq_sb[0:1, :],
                            ALU.mult, ALU.add).then_inc(c_ev.sem, 1)
                        v.wait_ge(c_ev.sem, CPP(hh))
                        if hh >= 1:
                            v.wait_ge(c_amm.sem,
                                      amm0 + last_ti_of_head[hh - 1] + 1)
                        v.tensor_scalar_mul(crb[0:1, :], rows[0:1, 0:4096],
                                            -1.0).then_inc(c_ev.sem, 1)
                    qs = qs_of(qb, kc)
                    if kc >= 4 * qb:
                        v.wait_ge(c_amm.sem, amm0 + ti + 1)
                        v.tensor_add(ps_sc[ti % 3][:, qs:qs + 128],
                                     ps_sc[ti % 3][:, qs:qs + 128],
                                     mtri[:, :]).then_inc(c_msk.sem, 1)
                    if kc == 4 * qb + 3:
                        v.wait_ge(c_adn.sem, adn0 + last_ti_of_g[g] + 1)
                        dvslot = 512 * (qb % 2)
                        if g >= 2:
                            v.wait_ge(*SDV_TOK(g - 2))
                        v.tensor_scalar_max(rows[0:1, dvslot:dvslot + 512],
                                            ps_dn[qb % 2][0:1, :], 1e-30) \
                            .then_inc(c_ev.sem, 1)
                        v.wait_ge(c_ev.sem, CLAMP(hh, qb))
                        v.reciprocal(rows[0:1, dvslot:dvslot + 512],
                                     rows[0:1, dvslot:dvslot + 512]) \
                            .then_inc(c_ev.sem, 1)
                        pend_on.append(g)
                        if len(pend_on) > 1:
                            gp = pend_on.pop(0)
                            hp, qp = divmod(gp, 8)
                            v.wait_ge(c_apv.sem,
                                      apv0 + last_ti_of_g[gp] + 1)
                            v.wait_ge(*SDB_TOK(gp))
                            if gp >= 2:
                                v.wait_ge(*SOB_TOK(gp - 2))
                            v.tensor_mul(obf[gp % 2][:, :],
                                         ps_o[gp % 2][:, :],
                                         dbb[gp % 2][:, :]) \
                                .then_inc(c_aon.sem, 1)
                for gp in pend_on:
                    hp, qp = divmod(gp, 8)
                    v.wait_ge(c_apv.sem, apv0 + last_ti_of_g[gp] + 1)
                    v.wait_ge(*SDB_TOK(gp))
                    v.wait_ge(*SOB_TOK(gp - 2))
                    v.tensor_mul(obf[gp % 2][:, :], ps_o[gp % 2][:, :],
                                 dbb[gp % 2][:, :]).then_inc(c_aon.sem, 1)
            c_ev.n += 36 * HPC
            c_msk.n += mcount[-1]
            c_aon.n += 64

            @block.sync
            def _(s):
                for hh in range(HPC):
                    if hh >= 1:
                        s.wait_ge(c_ev.sem, CPP(hh - 1))
                    s.dma_start(out=aq_sb[0:1, :],
                                in_=alibiq[0:1, S * hh:S * (hh + 1)]) \
                        .then_inc(s_aq.sem, 16)
                    for qb in range(8):
                        g = 8 * hh + qb
                        dvslot = 512 * (qb % 2)
                        s.wait_ge(c_ev.sem, RECIP(hh, qb))
                        if g >= 2:
                            s.wait_ge(*SDB_TOK(g - 2))
                        s.dma_start(
                            out=rsc[0:1, RO_DNV + 512 * (g % 2):
                                    RO_DNV + 512 * (g % 2) + 512],
                            in_=rows[0:1, dvslot:dvslot + 512]) \
                            .then_inc(sdv[g % 2].sem, 16)
                        s.wait_ge(*SDV_TOK(g))
                        if g >= 2:
                            s.wait_ge(c_aon.sem, aon0 + g - 1)
                        s.dma_start(
                            out=dbb[g % 2][:, :],
                            in_=rsc[0:1, RO_DNV + 512 * (g % 2):
                                    RO_DNV + 512 * (g % 2) + 512]
                            .partition_broadcast(128)).then_inc(sdb[g % 2].sem, 16)
                        if g >= 1:
                            hp, qp = divmod(g - 1, 8)
                            s.wait_ge(c_aon.sem, aon0 + g)
                            dstO = (aoiA[128 * hp:128 * (hp + 1),
                                          512 * qp:512 * (qp + 1)]
                                    if hp < 7 else
                                    aoiB[0:128, 512 * qp:512 * (qp + 1)])
                            s.dma_start(out=dstO,
                                        in_=obf[(g - 1) % 2][:, :]) \
                                .then_inc(sob[(g - 1) % 2].sem, 16)
                s.wait_ge(c_aon.sem, aon0 + 64)
                s.dma_start(out=aoiB[0:128, 512 * 7:512 * 8],
                            in_=obf[63 % 2][:, :]).then_inc(sob[63 % 2].sem, 16)
            s_aq.n += 16 * HPC
            for g in range(64):
                sdv[g % 2].n += 16
                sdb[g % 2].n += 16
                sob[g % 2].n += 16

        phase_fence()

        phD = contextlib.ExitStack()
        with phD:
            abuf = phD.enter_context(
                nc.sbuf_tensor("abuf", [128, NCH * NT], BF16))
            xin = [phD.enter_context(nc.sbuf_tensor(f"xiD{i}", [128, NT], F32))
                   for i in range(3)]
            xo_sb = [phD.enter_context(nc.sbuf_tensor(f"xqD{i}", [128, NT], F32))
                     for i in range(2)]
            wslab = [phD.enter_context(
                nc.sbuf_tensor(f"wbD{i}", [128, NCH * 128], BF16))
                for i in range(2)]
            ps_m = [phD.enter_context(nc.psum_tensor(f"pDm{i}", [128, NT], F32))
                    for i in range(2)]

            aooA_v = aooA.rearrange("f (tb t) -> (f tb) t", tb=4)
            aooB_v = aooB.rearrange("f (tb t) -> (f tb) t", tb=4)
            gl1 = c_gl.n
            mmD = c_mm.n
            swlD = slot_bases(swl)

            @block.gpsimd
            def _(g):
                for m in range(2):
                    g.dma_start(
                        out=wslab[m % 2][:, :].rearrange("p (c f) -> p c f",
                                                         c=NCH),
                        in_=woutT[:, 128 * m:128 * (m + 1)].rearrange(
                            "(c p) f -> p c f", p=128)
                    ).then_inc(swl[m % 2].sem, 16)
                g.wait_ge(c_cc.sem, T_AGO1)
                for kc in range(NCH):
                    if kc == 28:
                        g.wait_ge(c_cc.sem, T_AGO2)
                    g.indirect_dma_start(
                        out=abuf[:, NT * kc:NT * (kc + 1)],
                        out_offset=None, in_=(aooA_v if kc < 28 else aooB_v),
                        in_offset=IndirectOffsetOnAxis(
                            ap=oidx_sb[:, kc:kc + 1], axis=0)) \
                        .then_inc(c_gl.sem, 16)
                for m in range(2, NCH):
                    if m >= 2:
                        g.wait_ge(c_mm.sem, mmD + m - 1)
                    g.dma_start(
                        out=wslab[m % 2][:, :].rearrange("p (c f) -> p c f",
                                                         c=NCH),
                        in_=woutT[:, 128 * m:128 * (m + 1)].rearrange(
                            "(c p) f -> p c f", p=128)
                    ).then_inc(swl[m % 2].sem, 16)
            c_gl.n += 16 * NCH
            slot_bump(swl, 2, NCH)

            evD = c_ev.n
            sxbD = slot_bases(sx)
            ssdD = slot_bases(ssd)

            @block.tensor
            def _(t):
                t.wait_ge(c_gl.sem, gl1 + 16 * NCH)
                for m in range(NCH):
                    t.wait_ge(*slot_tok(swl, swlD, 2, m))
                    if m >= 2:
                        t.wait_ge(c_ev.sem, evD + m - 1)
                    for kc in range(NCH):
                        for hf in range(2):
                            last = t.matmul(
                                ps_m[m % 2][:, 512 * hf:512 * (hf + 1)],
                                wslab[m % 2][:, 128 * kc:128 * (kc + 1)],
                                abuf[:, NT * kc + 512 * hf:
                                     NT * kc + 512 * (hf + 1)],
                                start=(kc == 0), stop=(kc == NCH - 1))
                    last.then_inc(c_mm.sem, 1)
            c_mm.n += NCH

            @block.sync
            def _(s):
                for m in range(NCH + 2):
                    if m < NCH:
                        if m >= 3:
                            s.wait_ge(c_ev.sem, evD + m - 2)
                        s.dma_start(out=xin[m % 3][:, :],
                                    in_=xres1[128 * m:128 * (m + 1), :]) \
                            .then_inc(sx[m % 3].sem, 16)
                    if m >= 2:
                        s.wait_ge(c_ev.sem, evD + m - 1)
                        s.dma_start(out=xres2[128 * (m - 2):128 * (m - 1), :],
                                    in_=xo_sb[m % 2][:, :]) \
                            .then_inc(ssd[m % 2].sem, 16)
            slot_bump(sx, 3, NCH)
            slot_bump(ssd, 2, NCH)

            @block.vector
            def _(v):
                for m in range(NCH):
                    v.wait_ge(c_mm.sem, mmD + m + 1)
                    v.wait_ge(*slot_tok(sx, sxbD, 3, m))
                    if m >= 2:
                        v.wait_ge(*slot_tok(ssd, ssdD, 2, m - 2))
                    v.tensor_add(xo_sb[m % 2][:, :], ps_m[m % 2][:, :],
                                 xin[m % 3][:, :]).then_inc(c_ev.sem, 1)
            c_ev.n += NCH



        phase_fence()

        emit_ffn(xres2, outT, 1, False, "e")

        fin = [(s.sem, s.n) for s in ssd]

        @block.sync
        def _(s):
            for sm, vv in fin:
                s.wait_ge(sm, vv)

        @block.gpsimd
        def _(g):
            for sm, vv in fin:
                g.wait_ge(sm, vv)

        nc._dbg_counts = {c.sem.name if hasattr(c.sem, 'name') else str(i):
                          (c.sem, c.n) for i, c in enumerate(ALL_CTRS)}
    return nc



_CACHE = {}


def _permute_wout(w):
    order = [8 * (c // 7) + c % 7 for c in range(28)] + \
        [8 * c + 7 for c in range(4)]
    blocks = [w[128 * H:128 * (H + 1), :] for H in order]
    return np.ascontiguousarray(np.concatenate(blocks, axis=0))


def _prep(inp):
    f32 = lambda a: np.asarray(a, dtype=np.float32)
    x = np.ascontiguousarray(inp["L0_attn_hidden_states"], dtype=np.float32)
    wqT = np.ascontiguousarray(f32(inp["L0_attn_w_qkv"]).T)
    qln = tuple(f32(inp[k]) for k in ("L0_attn_q_ln_w", "L0_attn_q_ln_b",
                                      "L0_attn_k_ln_w", "L0_attn_k_ln_b"))
    shared = {
        "woutT": _permute_wout(np.ascontiguousarray(
            f32(inp["L0_attn_w_out"]).T)),
        "ffnWT": np.ascontiguousarray(f32(inp["L0_ffn_weight"]).T),
        "loraAT": np.ascontiguousarray(
            f32(inp["L0_ffn_lora_A"]).transpose(1, 0, 2).reshape(D, E * R)),
        "loraBs": np.ascontiguousarray(
            f32(inp["L0_ffn_lora_B"]).reshape(E * R, D)),
        "gw1T": np.ascontiguousarray(f32(inp["L0_ffn_gw1"]).T),
        "gw2T": np.ascontiguousarray(f32(inp["L0_ffn_gw2"]).T),
        "gvec": np.ascontiguousarray(np.stack(
            [f32(inp["L0_ffn_gb1"]), f32(inp["L0_ffn_ln_gamma"]),
             f32(inp["L0_ffn_ln_beta"])], axis=1)),
        "gb2r": np.ascontiguousarray(f32(inp["L0_ffn_gb2"]).reshape(1, E)),
    }
    return x, wqT, qln, shared


def _build_inputs(core, x, wqT, qln, shared):
    b, j = core // 4, core % 4
    m = dict(shared)
    m["xT"] = np.ascontiguousarray(
        x[b, NT * j:NT * (j + 1), :].T).astype(np.float32, copy=False)
    sl = slice(NT * j, NT * (j + 1))
    m["wqkvT"] = np.ascontiguousarray(np.concatenate(
        [wqT[:, D + NT * j:D + NT * (j + 1)],
         wqT[:, NT * j:NT * (j + 1)],
         wqT[:, 2 * D + NT * j:2 * D + NT * (j + 1)]], axis=1))
    qkln = np.zeros((128, 32), np.float32)
    for r in range(8):
        for c4 in range(4):
            qkln[:, 4 * r + c4] = qln[c4][NT * j + 128 * r:
                                          NT * j + 128 * (r + 1)]
    m["qkln"] = qkln
    p = np.arange(128)
    alibik = np.zeros((128, HPC * NCH), np.float32)
    alibiq = np.zeros((1, HPC * S), np.float32)
    for hh in range(HPC):
        h = 8 * j + hh
        slope = 2.0 ** (-(h + 1) / 4.0)
        for kc in range(NCH):
            alibik[:, NCH * hh + kc] = slope * (128 * kc + p)
        alibiq[0, S * hh:S * (hh + 1)] = \
            (slope * np.arange(S) + 1.0) / SCALE
    oidx = np.zeros((128, NCH), np.int32)
    for kc in range(NCH):
        c_ = kc if kc < 28 else kc - 28
        oidx[:, kc] = 4 * (128 * c_ + p) + j
    m["oidx"] = oidx
    m["alibik"] = alibik
    m["alibiq"] = alibiq
    return m


def kernel(L0_attn_hidden_states, L0_attn_w_qkv, L0_attn_q_ln_w,
           L0_attn_q_ln_b, L0_attn_k_ln_w, L0_attn_k_ln_b, L0_attn_w_out,
           L0_ffn_weight, L0_ffn_lora_A, L0_ffn_lora_B, L0_ffn_gw1,
           L0_ffn_gb1, L0_ffn_ln_gamma, L0_ffn_ln_beta, L0_ffn_gw2,
           L0_ffn_gb2):
    inp = dict(
        L0_attn_hidden_states=L0_attn_hidden_states,
        L0_attn_w_qkv=L0_attn_w_qkv, L0_attn_q_ln_w=L0_attn_q_ln_w,
        L0_attn_q_ln_b=L0_attn_q_ln_b, L0_attn_k_ln_w=L0_attn_k_ln_w,
        L0_attn_k_ln_b=L0_attn_k_ln_b, L0_attn_w_out=L0_attn_w_out,
        L0_ffn_weight=L0_ffn_weight, L0_ffn_lora_A=L0_ffn_lora_A,
        L0_ffn_lora_B=L0_ffn_lora_B, L0_ffn_gw1=L0_ffn_gw1,
        L0_ffn_gb1=L0_ffn_gb1, L0_ffn_ln_gamma=L0_ffn_ln_gamma,
        L0_ffn_ln_beta=L0_ffn_ln_beta, L0_ffn_gw2=L0_ffn_gw2,
        L0_ffn_gb2=L0_ffn_gb2)
    x, wqT, qln, shared = _prep(inp)
    if "nc" not in _CACHE:
        _CACHE["nc"] = build_program()
    nc = _CACHE["nc"]
    in_maps = [_build_inputs(c, x, wqT, qln, shared) for c in range(8)]
    res = run_bass_kernel_spmd(nc, in_maps, core_ids=list(range(8)))
    out = np.empty((2, S, D), np.float32)
    for c in range(8):
        b, j = c // 4, c % 4
        out[b, NT * j:NT * (j + 1), :] = res.results[c]["outT"].T
    return out


# version 11 (final)
# speedup vs baseline: 1.0019x; 1.0019x over previous
import numpy as np
import contextlib
import concourse.bass as bass
import concourse.mybir as mybir
from concourse.bass import IndirectOffsetOnAxis
from concourse.bass_utils import run_bass_kernel_spmd

F32 = mybir.dt.float32
BF16 = mybir.dt.bfloat16
I32 = mybir.dt.int32
AF = mybir.ActivationFunctionType
ALU = mybir.AluOpType
AX = mybir.AxisListType.X

D, NCH, NT, S, HPC = 4096, 32, 1024, 4096, 8
E, R, GH = 8, 16, 32
CAP, ALPHA = 3072.0, 1.0 / 16
SCALE = float(128) ** -0.5
LN_EPS = 1e-5
NEG_BIG = -1.0e30

RW_MU, RW_RS, RW_V, RW_MX = 0, NT, 2 * NT, 3 * NT
RW_KS, RW_QN = 4096, 8192 - 4096
RW_KS = 4096
RW_CPP = 0
RW_DNV = 0
RW_QNR = 4096

RO_LN, RO_G, RO_ALW = 0, 2 * NT, 3 * NT
RO_CPP, RO_DNV = 4096, 8192


class Ctr:
    __slots__ = ("sem", "n")


def build_program():
    nc = bass.Bass("TRN2", target_bir_lowering=False)

    xT = nc.dram_tensor("xT", [D, NT], F32, kind="ExternalInput").ap()
    wqkvT = nc.dram_tensor("wqkvT", [D, 3 * NT], F32, kind="ExternalInput").ap()
    woutT = nc.dram_tensor("woutT", [D, D], F32, kind="ExternalInput").ap()
    ffnWT = nc.dram_tensor("ffnWT", [D, D], F32, kind="ExternalInput").ap()
    loraAT = nc.dram_tensor("loraAT", [D, E * R], F32, kind="ExternalInput").ap()
    loraBs = nc.dram_tensor("loraBs", [E * R, D], F32, kind="ExternalInput").ap()
    gw1T = nc.dram_tensor("gw1T", [D, GH], F32, kind="ExternalInput").ap()
    gw2T = nc.dram_tensor("gw2T", [GH, E], F32, kind="ExternalInput").ap()
    qkln = nc.dram_tensor("qkln", [128, 32], F32, kind="ExternalInput").ap()
    gvec = nc.dram_tensor("gvec", [GH, 3], F32, kind="ExternalInput").ap()
    gb2r = nc.dram_tensor("gb2r", [1, E], F32, kind="ExternalInput").ap()
    alibik = nc.dram_tensor("alibik", [128, HPC * NCH], F32,
                            kind="ExternalInput").ap()
    alibiq = nc.dram_tensor("alibiq", [1, HPC * S], F32,
                            kind="ExternalInput").ap()
    oidx = nc.dram_tensor("oidx", [128, NCH], I32, kind="ExternalInput").ap()

    outT = nc.dram_tensor("outT", [D, NT], F32, kind="ExternalOutput").ap()

    xres1 = nc.dram_tensor("xres1", [D, NT], F32, kind="Internal").ap()
    xres2 = nc.dram_tensor("xres2", [D, NT], F32, kind="Internal").ap()
    _YG = [4, 4, 8, 8, 8]
    agy_i = [nc.dram_tensor(f"agy_i{a}", [128 * _YG[a], NT], BF16,
                            kind="Internal").ap() for a in range(5)]
    agy_o = [nc.dram_tensor(f"agy_o{a}", [4 * 128 * _YG[a], NT], BF16,
                            kind="Internal").ap() for a in range(5)]
    k_raw = nc.dram_tensor("k_raw", [NT, S], BF16, kind="Internal").ap()
    q_raw = nc.dram_tensor("q_raw", [NT, S], BF16, kind="Internal").ap()
    v_scr = nc.dram_tensor("v_scr", [NT, S], BF16, kind="Internal").ap()
    ka_scr = nc.dram_tensor("ka_scr", [NT, S], BF16, kind="Internal").ap()
    qa_scr = nc.dram_tensor("qa_scr", [NT, S], BF16, kind="Internal").ap()
    arki = nc.dram_tensor("arki", [1, 8192], F32, kind="Internal").ap()
    arko = nc.dram_tensor("arko", [1, 8192], F32, kind="Internal").ap()
    arqi = nc.dram_tensor("arqi", [1, 8192], F32, kind="Internal").ap()
    arqo = nc.dram_tensor("arqo", [1, 8192], F32, kind="Internal").ap()
    aoiA = nc.dram_tensor("aoiA", [7 * 128, S], BF16, kind="Internal").ap()
    aoiB = nc.dram_tensor("aoiB", [1 * 128, S], BF16, kind="Internal").ap()
    aooA = nc.dram_tensor("aooA", [28 * 128, S], BF16, kind="Internal").ap()
    aooB = nc.dram_tensor("aooB", [4 * 128, S], BF16, kind="Internal").ap()
    cin = [nc.dram_tensor(f"cin{i}", [1, 16], F32, kind="Internal").ap()
           for i in range(2)]
    cout = [nc.dram_tensor(f"cout{i}", [1, 16], F32, kind="Internal").ap()
            for i in range(2)]
    rsc = nc.dram_tensor("rsc", [1, 16384], F32, kind="Internal").ap()

    es = contextlib.ExitStack()
    with es:
        ident = es.enter_context(nc.sbuf_tensor("ident", [128, 128], BF16))
        mtri = es.enter_context(nc.sbuf_tensor("mtri", [128, 128], F32))
        ones_bc = es.enter_context(nc.sbuf_tensor("ones_bc", [128, 1], BF16))
        ones_r = es.enter_context(nc.sbuf_tensor("ones_r", [1, 128], BF16))
        ones_fc = es.enter_context(nc.sbuf_tensor("ones_fc", [128, 1], F32))
        ones32f = es.enter_context(nc.sbuf_tensor("ones32f", [GH, 1], F32))
        ones32b = es.enter_context(nc.sbuf_tensor("ones32b", [GH, 1], BF16))
        qkln_sb = es.enter_context(nc.sbuf_tensor("qkln_sb", [128, 32], F32))
        gvec_sb = es.enter_context(nc.sbuf_tensor("gvec_sb", [GH, 3], F32))
        gb2B = es.enter_context(nc.sbuf_tensor("gb2B", [128, E], F32))
        gw1_sb = es.enter_context(nc.sbuf_tensor("gw1_sb", [128, NCH * GH], BF16))
        gw2_sb = es.enter_context(nc.sbuf_tensor("gw2_sb", [GH, E], BF16))
        repl_sb = es.enter_context(nc.sbuf_tensor("repl_sb", [E, 128], BF16))
        lA_sb = es.enter_context(nc.sbuf_tensor("lA_sb", [128, NCH * 128], BF16))
        lB_sb = es.enter_context(nc.sbuf_tensor("lB_sb", [128, D], BF16))
        alibik_sb = es.enter_context(
            nc.sbuf_tensor("alibik_sb", [128, HPC * NCH], F32))
        oidx_sb = es.enter_context(nc.sbuf_tensor("oidx_sb", [128, NCH], I32))
        bmu = es.enter_context(nc.sbuf_tensor("bmu", [128, NT], F32))
        brstd = es.enter_context(nc.sbuf_tensor("brstd", [128, NT], F32))
        rows = es.enter_context(nc.sbuf_tensor("rows", [1, 8192], F32))
        epsc = es.enter_context(nc.sbuf_tensor("epsc", [1, 1], F32))

        def C(name):
            c = Ctr()
            c.sem = es.enter_context(nc.semaphore(name))
            c.n = 0
            return c

        c_set, c_pre = C("c_set"), C("c_pre")
        c_x, c_xu = C("c_x"), C("c_xu")
        c_wl, c_wu = C("c_wl"), C("c_wu")
        c_tbu = C("c_tbu")
        c_mm, c_ev, c_ex = C("c_mm"), C("c_ev"), C("c_ex")
        c_row, c_sc, c_cc, c_gl = C("c_row"), C("c_sc"), C("c_cc"), C("c_gl")
        c_amm, c_asu, c_aex = C("c_amm"), C("c_asu"), C("c_aex")
        c_apv, c_aac, c_aon, c_adn = C("c_apv"), C("c_aac"), C("c_aon"), C("c_adn")
        c_msk = C("c_msk")
        sx = [C(f"sx{i}") for i in range(3)]
        stb = [C(f"stb{i}") for i in range(2)]
        swl = [C(f"swl{i}") for i in range(2)]
        sst = [C(f"sst{i}") for i in range(2)]
        ssd = [C(f"ssd{i}") for i in range(2)]
        skh = [C(f"skh{i}") for i in range(2)]
        sqh = [C(f"sqh{i}") for i in range(2)]
        svt = C("svt")
        sob = [C(f"sob{i}") for i in range(2)]
        sdv = [C(f"sdv{i}") for i in range(2)]
        sdb = [C(f"sdb{i}") for i in range(2)]
        s_aq, s_cpb, s_cbb = C("s_aq"), C("s_cpb"), C("s_cbb")
        s_lnr, s_cnt = C("s_lnr"), C("s_cnt")
        sqa = [C(f"sqa{i}") for i in range(2)]
        sya = [C(f"sya{i}") for i in range(2)]
        c_ybr = C("c_ybr")
        s_bnc = C("s_bnc")
        syb = [C(f"syb{i}") for i in range(5)]
        s_ar = C("s_ar")

        def slot_tok(group, base, k, idx):
            s = group[idx % k]
            return s.sem, base[idx % k] + 16 * (idx // k + 1)

        def slot_bases(group):
            return [s.n for s in group]

        def slot_bump(group, k, nuses):
            for i in range(nuses):
                group[i % k].n += 16

        block = es.enter_context(nc.Block())

        @block.gpsimd
        def _(g):
            for dst, src in [
                (qkln_sb[:, :], qkln), (gvec_sb[:, :], gvec),
                (gw2_sb[:, :], gw2T), (lB_sb[:, :], loraBs),
                (alibik_sb[:, :], alibik),
                (oidx_sb[:, :], oidx),
            ]:
                g.dma_start(out=dst, in_=src).then_inc(c_set.sem, 16)
            g.dma_start(
                out=gw1_sb[:, :].rearrange("p (c gh) -> p c gh", c=NCH),
                in_=gw1T.rearrange("(c p) gh -> p c gh", p=128)
            ).then_inc(c_set.sem, 16)
            g.dma_start(
                out=lA_sb[:, :].rearrange("p (c f) -> p c f", c=NCH),
                in_=loraAT.rearrange("(c p) f -> p c f", p=128)
            ).then_inc(c_set.sem, 16)
            g.dma_start(out=gb2B[:, :],
                        in_=gb2r.partition_broadcast(128)).then_inc(c_set.sem, 16)
            g.memset(ident[:, :], 1.0).then_inc(c_pre.sem, 1)
            g.memset(mtri[:, :], 0.0).then_inc(c_pre.sem, 1)
            g.memset(ones_bc[:, :], 1.0).then_inc(c_pre.sem, 1)
            g.memset(ones_r[:, :], 1.0).then_inc(c_pre.sem, 1)
            g.memset(ones_fc[:, :], 1.0).then_inc(c_pre.sem, 1)
            g.memset(ones32f[:, :], 1.0).then_inc(c_pre.sem, 1)
            g.memset(ones32b[:, :], 1.0).then_inc(c_pre.sem, 1)
            g.memset(repl_sb[:, :], 1.0).then_inc(c_pre.sem, 1)
            g.memset(epsc[:, :], LN_EPS).then_inc(c_pre.sem, 1)
            g.wait_ge(c_pre.sem, 9)
            g.affine_select(ident[:, :], ident[:, :], pattern=[[-1, 128]],
                            compare_op=ALU.is_equal, fill=0.0, base=0,
                            channel_multiplier=1).then_inc(c_pre.sem, 1)
            g.affine_select(mtri[:, :], mtri[:, :], pattern=[[1, 128]],
                            compare_op=ALU.is_ge, fill=NEG_BIG, base=0,
                            channel_multiplier=-1).then_inc(c_pre.sem, 1)
            g.affine_select(repl_sb[:, :], repl_sb[:, :], pattern=[[1, 128]],
                            compare_op=ALU.is_ge, fill=0.0, base=0,
                            channel_multiplier=-16).then_inc(c_pre.sem, 1)
            g.wait_ge(c_pre.sem, 12)
            g.affine_select(repl_sb[:, :], repl_sb[:, :], pattern=[[-1, 128]],
                            compare_op=ALU.is_ge, fill=0.0, base=15,
                            channel_multiplier=16).then_inc(c_pre.sem, 1)
        c_set.n += 16 * 9
        c_pre.n += 13
        T_SET, T_PRE = c_set.n, c_pre.n

        def ln_finalize(ps1, ps2, mm_tok, denom, roff, dst_mu, dst_rstd, nbc,
                        extra_bcast_ev_wait=None):
            evb = c_ev.n
            prev_bounce = s_lnr.n

            @block.vector
            def _(v):
                v.wait_ge(mm_tok[0], mm_tok[1])
                v.wait_ge(s_lnr.sem, prev_bounce)
                v.tensor_scalar_mul(rows[0:1, RW_MU:RW_MU + NT], ps1[0:1, :],
                                    1.0 / denom).then_inc(c_ev.sem, 1)
                v.wait_ge(c_ev.sem, evb + 1)
                v.tensor_mul(rows[0:1, RW_V:RW_V + NT],
                             rows[0:1, RW_MU:RW_MU + NT],
                             rows[0:1, RW_MU:RW_MU + NT]).then_inc(c_ev.sem, 1)
                v.wait_ge(c_ev.sem, evb + 2)
                v.scalar_tensor_tensor(
                    rows[0:1, RW_V:RW_V + NT], ps2[0:1, :], 1.0 / denom,
                    rows[0:1, RW_V:RW_V + NT], ALU.mult, ALU.subtract) \
                    .then_inc(c_ev.sem, 1)
            c_ev.n += 3
            t_var = c_ev.n

            @block.scalar
            def _(sc):
                sc.wait_ge(c_ev.sem, t_var)
                sc.activation(rows[0:1, RW_RS:RW_RS + NT],
                              rows[0:1, RW_V:RW_V + NT],
                              AF.Sqrt, bias=epsc[0:1, 0:1]).then_inc(c_ex.sem, 1)
            c_ex.n += 1
            t_sq = c_ex.n

            @block.vector
            def _(v):
                v.wait_ge(c_ex.sem, t_sq)
                v.reciprocal(rows[0:1, RW_RS:RW_RS + NT],
                             rows[0:1, RW_RS:RW_RS + NT]).then_inc(c_ev.sem, 1)
            c_ev.n += 1
            t_rec = c_ev.n

            rb = s_lnr.n

            @block.sync
            def _(s):
                s.wait_ge(c_ev.sem, t_rec)
                if extra_bcast_ev_wait is not None:
                    s.wait_ge(c_ev.sem, extra_bcast_ev_wait)
                s.dma_start(out=rsc[0:1, roff:roff + 2 * NT],
                            in_=rows[0:1, RW_MU:RW_MU + 2 * NT]) \
                    .then_inc(s_lnr.sem, 16)
                s.wait_ge(s_lnr.sem, rb + 16)
                s.dma_start(out=dst_mu,
                            in_=rsc[0:1, roff:roff + NT]
                            .partition_broadcast(nbc)).then_inc(s_lnr.sem, 16)
                s.dma_start(out=dst_rstd,
                            in_=rsc[0:1, roff + NT:roff + 2 * NT]
                            .partition_broadcast(nbc)).then_inc(s_lnr.sem, 16)
            s_lnr.n += 48
            return s_lnr.n, t_var

        ALL_CTRS = [c_set, c_pre, c_tbu, c_mm, c_ev, c_ex, c_cc, c_gl,
                    c_amm, c_asu, c_aex, c_apv, c_aac, c_aon, c_adn, c_msk,
                    s_lnr, s_cnt, s_aq, s_cpb, s_cbb, svt, s_ar] + \
            sx + stb + swl + sst + ssd + skh + sqh + sob + sdv + sdb + \
            sqa + sya + syb + [s_bnc, c_ybr]

        def phase_fence():
            vals = [(c.sem, c.n) for c in ALL_CTRS if c.n > 0]

            @block.gpsimd
            def _(g):
                for sm, vv in vals:
                    g.wait_ge(sm, vv)

            @block.tensor
            def _(t):
                for sm, vv in vals:
                    t.wait_ge(sm, vv)

            @block.vector
            def _(v):
                for sm, vv in vals:
                    v.wait_ge(sm, vv)

            @block.scalar
            def _(sc):
                for sm, vv in vals:
                    sc.wait_ge(sm, vv)

            @block.sync
            def _(s):
                for sm, vv in vals:
                    s.wait_ge(sm, vv)

        def emit_ffn(src, dst, cidx, next_stats, tag):
            ph = contextlib.ExitStack()
            with ph:
                xfb = ph.enter_context(
                    nc.sbuf_tensor(f"xf{tag}", [128, NCH * NT], BF16))
                xin = [ph.enter_context(
                    nc.sbuf_tensor(f"xi{i}{tag}", [128, NT], F32))
                    for i in range(3)]
                xo_sb = [ph.enter_context(
                    nc.sbuf_tensor(f"xq{i}{tag}", [128, NT], F32))
                    for i in range(2)]
                tb = [ph.enter_context(
                    nc.sbuf_tensor(f"tc{i}{tag}", [128, NT], BF16))
                    for i in range(4)]
                wslab = [ph.enter_context(
                    nc.sbuf_tensor(f"wb{i}{tag}", [128, NCH * 128], BF16))
                    for i in range(2)]
                h_sb = ph.enter_context(nc.sbuf_tensor(f"h{tag}", [GH, NT], F32))
                h2_sb = ph.enter_context(nc.sbuf_tensor(f"h2{tag}", [GH, NT], F32))
                hb_sb = ph.enter_context(
                    nc.sbuf_tensor(f"hb{tag}", [GH, 2 * NT], BF16))
                hr_sb = ph.enter_context(nc.sbuf_tensor(f"hr{tag}", [GH, NT], BF16))
                gbmu = ph.enter_context(nc.sbuf_tensor(f"gm{tag}", [GH, NT], F32))
                gbrs = ph.enter_context(nc.sbuf_tensor(f"gr{tag}", [GH, NT], F32))
                gat = ph.enter_context(nc.sbuf_tensor(f"ga{tag}", [128, 8 * E], F32))
                gng = ph.enter_context(nc.sbuf_tensor(f"gg{tag}", [128, E], F32))
                msk = ph.enter_context(
                    nc.sbuf_tensor(f"mk{tag}", [128, 8 * 2 * E], F32))
                mcl = ph.enter_context(nc.sbuf_tensor(f"mc{tag}", [128, 32], F32))
                w_sb = ph.enter_context(nc.sbuf_tensor(f"w{tag}", [128, 8 * E], BF16))
                wT_sb = ph.enter_context(nc.sbuf_tensor(f"wt{tag}", [E, NT], BF16))
                cnt_sb = ph.enter_context(nc.sbuf_tensor(f"ct{tag}", [1, 16], F32))
                alw_sb = ph.enter_context(nc.sbuf_tensor(f"aw{tag}", [128, 16], F32))
                tmp_sb = ph.enter_context(nc.sbuf_tensor(f"tp{tag}", [128, NT], F32))
                tmps = ph.enter_context(nc.sbuf_tensor(f"tz{tag}", [128, NT], BF16))
                ps_s1 = ph.enter_context(nc.psum_tensor(f"p1{tag}", [1, NT], F32))
                ps_s2 = ph.enter_context(nc.psum_tensor(f"p2{tag}", [1, NT], F32))

                xub, tbu = c_xu.n, c_tbu.n
                exb = c_ex.n
                sxb = slot_bases(sx)
                ssd_tot = [(s.sem, s.n) for s in ssd]

                @block.sync
                def _(s):
                    s.wait_ge(c_pre.sem, T_PRE)
                    for sm, vv in ssd_tot:
                        s.wait_ge(sm, vv)
                    for kc in range(NCH):
                        if kc >= 3:
                            s.wait_ge(c_tbu.sem, tbu + kc - 2)
                        s.dma_start(out=xin[kc % 3][:, :],
                                    in_=src[128 * kc:128 * (kc + 1), :]) \
                            .then_inc(sx[kc % 3].sem, 16)
                slot_bump(sx, 3, NCH)

                @block.scalar
                def _(sc):
                    sc.wait_ge(c_set.sem, T_SET)
                    for kc in range(NCH):
                        sc.wait_ge(*slot_tok(sx, sxb, 3, kc))
                        if kc >= 2:
                            sc.wait_ge(c_tbu.sem, tbu + kc - 1)
                        sc.copy(tb[kc % 2][:, :], xin[kc % 3][:, :]) \
                            .then_inc(c_ex.sem, 1)
                        sc.square(tb[2 + kc % 2][:, :], xin[kc % 3][:, :]) \
                            .then_inc(c_ex.sem, 1)
                c_ex.n += 2 * NCH

                @block.tensor
                def _(t):
                    for kc in range(NCH):
                        t.wait_ge(c_ex.sem, exb + 2 * (kc + 1))
                        for hf in range(2):
                            sl = slice(512 * hf, 512 * (hf + 1))
                            t.matmul(ps_s1[0:1, sl], ones_bc[:, :],
                                     tb[kc % 2][:, sl], start=(kc == 0),
                                     stop=(kc == NCH - 1))
                            last = t.matmul(ps_s2[0:1, sl], ones_bc[:, :],
                                            tb[2 + kc % 2][:, sl],
                                            start=(kc == 0),
                                            stop=(kc == NCH - 1))
                        last.then_inc(c_tbu.sem, 1)
                c_tbu.n += NCH

                t_bc, t_ps_free = ln_finalize(ps_s1, ps_s2,
                                              (c_tbu.sem, c_tbu.n), D, RO_LN,
                                              bmu[:, :], brstd[:, :], 128)

                xub2 = c_xu.n
                evb = c_ev.n
                sxb2 = slot_bases(sx)

                @block.sync
                def _(s):
                    s.wait_ge(s_lnr.sem, t_bc)
                    for kc in range(NCH):
                        if kc >= 3:
                            s.wait_ge(c_ev.sem, evb + 2 * (kc - 3) + 2)
                        s.dma_start(out=xin[kc % 3][:, :],
                                    in_=src[128 * kc:128 * (kc + 1), :]) \
                            .then_inc(sx[kc % 3].sem, 16)
                slot_bump(sx, 3, NCH)

                @block.vector
                def _(v):
                    v.wait_ge(s_lnr.sem, t_bc)
                    for kc in range(NCH):
                        v.wait_ge(*slot_tok(sx, sxb2, 3, kc))
                        if kc >= 1:
                            v.wait_ge(c_ev.sem, evb + 2 * kc)
                        v.tensor_sub(xo_sb[0][:, :], xin[kc % 3][:, :],
                                     bmu[:, :]).then_inc(c_ev.sem, 1)
                        v.wait_ge(c_ev.sem, evb + 2 * kc + 1)
                        v.tensor_mul(xfb[:, NT * kc:NT * (kc + 1)],
                                     xo_sb[0][:, :], brstd[:, :]) \
                            .then_inc(c_ev.sem, 1)
                c_ev.n += 2 * NCH
                t_xf = c_ev.n

                st1 = contextlib.ExitStack()
                with st1:
                    ps_u = st1.enter_context(
                        nc.psum_tensor(f"pu{tag}", [128, NT], F32))
                    ps_g = st1.enter_context(
                        nc.psum_tensor(f"pg{tag}", [128, 512], F32))

                    @block.tensor
                    def _(t):
                        t.wait_ge(c_ev.sem, t_xf)
                        for kc in range(NCH):
                            for hf in range(2):
                                last = t.matmul(
                                    ps_u[0:GH, 512 * hf:512 * (hf + 1)],
                                    gw1_sb[:, GH * kc:GH * (kc + 1)],
                                    xfb[:, NT * kc + 512 * hf:
                                        NT * kc + 512 * (hf + 1)],
                                    start=(kc == 0), stop=(kc == NCH - 1))
                        last.then_inc(c_mm.sem, 1)
                    c_mm.n += 1
                    t_hmm = c_mm.n

                    @block.vector
                    def _(v):
                        v.wait_ge(c_mm.sem, t_hmm)
                        v.tensor_scalar(h_sb[:, :], ps_u[0:GH, :],
                                        gvec_sb[:, 0:1], None, ALU.add) \
                            .then_inc(c_ev.sem, 1)
                    c_ev.n += 1
                    t_h = c_ev.n

                    @block.scalar
                    def _(sc):
                        sc.wait_ge(c_ev.sem, t_h)
                        sc.copy(hb_sb[:, 0:NT], h_sb[:, :]).then_inc(c_ex.sem, 1)
                        sc.square(hb_sb[:, NT:2 * NT], h_sb[:, :]) \
                            .then_inc(c_ex.sem, 1)
                    c_ex.n += 2
                    t_hb = c_ex.n

                    @block.tensor
                    def _(t):
                        t.wait_ge(c_ex.sem, t_hb)
                        t.wait_ge(c_ev.sem, t_ps_free)
                        for hf in range(2):
                            sl = slice(512 * hf, 512 * (hf + 1))
                            t.matmul(ps_s1[0:1, sl], ones32b[:, :],
                                     hb_sb[:, sl], start=True, stop=True)
                            last = t.matmul(
                                ps_s2[0:1, sl], ones32b[:, :],
                                hb_sb[:, NT + 512 * hf:NT + 512 * (hf + 1)],
                                start=True, stop=True)
                        last.then_inc(c_mm.sem, 1)
                    c_mm.n += 1

                    t_gbc, t_gvar = ln_finalize(ps_s1, ps_s2,
                                                (c_mm.sem, c_mm.n),
                                                GH, RO_G, gbmu[:, :],
                                                gbrs[:, :], GH)

                    @block.vector
                    def _(v):
                        v.wait_ge(s_lnr.sem, t_gbc)
                        v.tensor_sub(h2_sb[:, :], h_sb[:, :], gbmu[:, :]) \
                            .then_inc(c_ev.sem, 1)
                        v.wait_ge(c_ev.sem, c_ev.n + 1)
                        v.tensor_mul(h2_sb[:, :], h2_sb[:, :], gbrs[:, :]) \
                            .then_inc(c_ev.sem, 1)
                    c_ev.n += 2
                    t_hn = c_ev.n

                    @block.scalar
                    def _(sc):
                        sc.wait_ge(c_ev.sem, t_hn)
                        sc.activation(hr_sb[:, :], h2_sb[:, :], AF.Relu,
                                      bias=gvec_sb[:, 2:3],
                                      scale=gvec_sb[:, 1:2]) \
                            .then_inc(c_ex.sem, 1)
                    c_ex.n += 1
                    t_hr = c_ex.n

                    gmm0 = c_mm.n
                    gv0 = c_ev.n

                    @block.tensor
                    def _(t):
                        t.wait_ge(c_ex.sem, t_hr)
                        for tc in range(8):
                            if tc >= 1:
                                t.wait_ge(c_ev.sem, gv0 + 7 * (tc - 1) + 1)
                            t.matmul(ps_g[:, 0:E],
                                     hr_sb[:, 128 * tc:128 * (tc + 1)],
                                     gw2_sb[:, :], start=True,
                                     stop=True).then_inc(c_mm.sem, 1)
                    c_mm.n += 8

                    @block.vector
                    def _(v):
                        for tc in range(8):
                            b = gv0 + 7 * tc
                            v.wait_ge(c_mm.sem, gmm0 + tc + 1)
                            v.tensor_add(gat[:, E * tc:E * (tc + 1)],
                                         ps_g[:, 0:E], gb2B[:, :]) \
                                .then_inc(c_ev.sem, 1)
                            v.wait_ge(c_ev.sem, b + 1)
                            v.reduce_max(mcl[:, 4 * tc:4 * tc + 1],
                                         gat[:, E * tc:E * (tc + 1)], axis=AX) \
                                .then_inc(c_ev.sem, 1)
                            v.wait_ge(c_ev.sem, b + 2)
                            v.tensor_scalar(msk[:, 16 * tc:16 * tc + E],
                                            gat[:, E * tc:E * (tc + 1)],
                                            mcl[:, 4 * tc:4 * tc + 1], None,
                                            ALU.is_equal).then_inc(c_ev.sem, 1)
                            v.wait_ge(c_ev.sem, b + 3)
                            v.scalar_tensor_tensor(
                                gng[:, :], msk[:, 16 * tc:16 * tc + E],
                                NEG_BIG, gat[:, E * tc:E * (tc + 1)],
                                ALU.mult, ALU.add).then_inc(c_ev.sem, 1)
                            v.wait_ge(c_ev.sem, b + 4)
                            v.reduce_max(mcl[:, 4 * tc + 1:4 * tc + 2],
                                         gng[:, :], axis=AX).then_inc(c_ev.sem, 1)
                            v.wait_ge(c_ev.sem, b + 5)
                            v.tensor_scalar(
                                msk[:, 16 * tc + E:16 * tc + 2 * E], gng[:, :],
                                mcl[:, 4 * tc + 1:4 * tc + 2], None,
                                ALU.is_equal).then_inc(c_ev.sem, 1)
                            v.wait_ge(c_ev.sem, b + 6)
                            v.tensor_sub(mcl[:, 4 * tc + 2:4 * tc + 3],
                                         mcl[:, 4 * tc:4 * tc + 1],
                                         mcl[:, 4 * tc + 1:4 * tc + 2]) \
                                .then_inc(c_ev.sem, 1)
                    c_ev.n += 7 * 8
                    t_top2 = c_ev.n

                    sg0 = c_ex.n

                    @block.scalar
                    def _(sc):
                        for tc in range(8):
                            sc.wait_ge(c_ev.sem, gv0 + 7 * tc + 7)
                            sc.activation(mcl[:, 4 * tc + 2:4 * tc + 3],
                                          mcl[:, 4 * tc + 2:4 * tc + 3],
                                          AF.Sigmoid).then_inc(c_ex.sem, 1)
                    c_ex.n += 8

                    @block.tensor
                    def _(t):
                        t.wait_ge(c_ev.sem, t_top2)
                        for tc in range(8):
                            last = t.matmul(ps_g[0:1, 0:16], ones_fc[:, :],
                                            msk[:, 16 * tc:16 * (tc + 1)],
                                            start=(tc == 0), stop=(tc == 7))
                        last.then_inc(c_mm.sem, 1)
                    c_mm.n += 1
                    t_cmm = c_mm.n

                    @block.vector
                    def _(v):
                        v.wait_ge(c_mm.sem, t_cmm)
                        v.tensor_copy(cnt_sb[:, :], ps_g[0:1, 0:16]) \
                            .then_inc(c_ev.sem, 1)
                    c_ev.n += 1
                    t_cnt = c_ev.n

                    rb1 = s_cnt.n

                    @block.sync
                    def _(s):
                        s.wait_ge(c_ev.sem, t_cnt)
                        s.dma_start(out=cin[cidx], in_=cnt_sb[:, :]) \
                            .then_inc(s_cnt.sem, 16)
                    s_cnt.n += 16

                    @block.gpsimd
                    def _(g):
                        g.wait_ge(s_cnt.sem, rb1 + 16)
                        g.collective_compute(
                            "AllReduce", ALU.add,
                            replica_groups=[list(range(8))],
                            ins=[cin[cidx].opt()], outs=[cout[cidx].opt()]) \
                            .then_inc(c_cc.sem, 1)
                    c_cc.n += 1
                    t_ar = c_cc.n

                    @block.sync
                    def _(s):
                        s.wait_ge(c_cc.sem, t_ar)
                        s.dma_start(out=cnt_sb[:, :], in_=cout[cidx]) \
                            .then_inc(s_cnt.sem, 16)
                    s_cnt.n += 16
                    t_cb = s_cnt.n

                    @block.vector
                    def _(v):
                        v.wait_ge(s_cnt.sem, t_cb)
                        v.tensor_scalar(cnt_sb[:, :], cnt_sb[:, :], CAP, ALPHA,
                                        ALU.is_le, ALU.mult) \
                            .then_inc(c_ev.sem, 1)
                    c_ev.n += 1
                    t_alw = c_ev.n

                    rb3 = s_cnt.n

                    @block.sync
                    def _(s):
                        s.wait_ge(c_ev.sem, t_alw)
                        s.dma_start(out=rsc[0:1, RO_ALW:RO_ALW + 16],
                                    in_=cnt_sb[:, :]).then_inc(s_cnt.sem, 16)
                        s.wait_ge(s_cnt.sem, rb3 + 16)
                        s.dma_start(out=alw_sb[:, :],
                                    in_=rsc[0:1, RO_ALW:RO_ALW + 16]
                                    .partition_broadcast(128)) \
                            .then_inc(s_cnt.sem, 16)
                    s_cnt.n += 32
                    t_alb = s_cnt.n

                    wv0 = c_ev.n

                    @block.vector
                    def _(v):
                        v.wait_ge(s_cnt.sem, t_alb)
                        for tc in range(8):
                            b = wv0 + 6 * tc
                            v.wait_ge(c_ex.sem, sg0 + tc + 1)
                            v.tensor_scalar(mcl[:, 4 * tc + 3:4 * tc + 4],
                                            mcl[:, 4 * tc + 2:4 * tc + 3],
                                            -1.0, 1.0, ALU.mult, ALU.add) \
                                .then_inc(c_ev.sem, 1)
                            v.tensor_scalar_mul(gat[:, E * tc:E * (tc + 1)],
                                                msk[:, 16 * tc:16 * tc + E],
                                                mcl[:, 4 * tc + 2:4 * tc + 3]) \
                                .then_inc(c_ev.sem, 1)
                            v.wait_ge(c_ev.sem, b + 2)
                            v.tensor_mul(gat[:, E * tc:E * (tc + 1)],
                                         gat[:, E * tc:E * (tc + 1)],
                                         alw_sb[:, 0:E]).then_inc(c_ev.sem, 1)
                            v.tensor_scalar_mul(
                                gng[:, :], msk[:, 16 * tc + E:16 * tc + 2 * E],
                                mcl[:, 4 * tc + 3:4 * tc + 4]) \
                                .then_inc(c_ev.sem, 1)
                            v.wait_ge(c_ev.sem, b + 4)
                            v.tensor_mul(gng[:, :], gng[:, :],
                                         alw_sb[:, E:2 * E]).then_inc(c_ev.sem, 1)
                            v.wait_ge(c_ev.sem, b + 5)
                            v.tensor_add(w_sb[:, E * tc:E * (tc + 1)],
                                         gat[:, E * tc:E * (tc + 1)],
                                         gng[:, :]).then_inc(c_ev.sem, 1)
                    c_ev.n += 6 * 8

                    wt0 = c_mm.n
                    wte0 = c_ev.n

                    @block.tensor
                    def _(t):
                        for tc in range(8):
                            t.wait_ge(c_ev.sem, wv0 + 6 * (tc + 1))
                            if tc >= 1:
                                t.wait_ge(c_ev.sem, wte0 + tc)
                            t.matmul(ps_g[0:E, 0:128],
                                     w_sb[:, E * tc:E * (tc + 1)],
                                     ident[:, :], start=True, stop=True) \
                                .then_inc(c_mm.sem, 1)
                    c_mm.n += 8

                    @block.vector
                    def _(v):
                        for tc in range(8):
                            v.wait_ge(c_mm.sem, wt0 + tc + 1)
                            v.tensor_copy(wT_sb[:, 128 * tc:128 * (tc + 1)],
                                          ps_g[0:E, 0:128]).then_inc(c_ev.sem, 1)
                    c_ev.n += 8
                    t_wT = c_ev.n

                    @block.tensor
                    def _(t):
                        t.wait_ge(c_ev.sem, t_h)
                        for kc in range(NCH):
                            for hf in range(2):
                                last = t.matmul(
                                    ps_u[:, 512 * hf:512 * (hf + 1)],
                                    lA_sb[:, 128 * kc:128 * (kc + 1)],
                                    xfb[:, NT * kc + 512 * hf:
                                        NT * kc + 512 * (hf + 1)],
                                    start=(kc == 0), stop=(kc == NCH - 1))
                        last.then_inc(c_mm.sem, 1)
                    c_mm.n += 1
                    t_tmm = c_mm.n

                    @block.vector
                    def _(v):
                        v.wait_ge(c_mm.sem, t_tmm)
                        v.tensor_copy(tmp_sb[:, :], ps_u[:, :]) \
                            .then_inc(c_ev.sem, 1)
                    c_ev.n += 1
                    t_tev = c_ev.n

                    we0 = c_mm.n
                    ts0 = c_ev.n

                    @block.tensor
                    def _(t):
                        t.wait_ge(c_ev.sem, t_wT)
                        for hf in range(2):
                            if hf == 1:
                                t.wait_ge(c_ev.sem, ts0 + 1)
                            t.matmul(ps_g[:, :], repl_sb[:, :],
                                     wT_sb[:, 512 * hf:512 * (hf + 1)],
                                     start=True, stop=True).then_inc(c_mm.sem, 1)
                    c_mm.n += 2

                    @block.vector
                    def _(v):
                        v.wait_ge(c_ev.sem, t_tev)
                        for hf in range(2):
                            v.wait_ge(c_mm.sem, we0 + hf + 1)
                            v.tensor_mul(tmps[:, 512 * hf:512 * (hf + 1)],
                                         tmp_sb[:, 512 * hf:512 * (hf + 1)],
                                         ps_g[:, :]).then_inc(c_ev.sem, 1)
                    c_ev.n += 2
                    t_tmps = c_ev.n

                st2 = contextlib.ExitStack()
                with st2:
                    ps_m = [st2.enter_context(
                        nc.psum_tensor(f"pm{i}{tag}", [128, NT], F32))
                        for i in range(2)]
                    wu0 = c_wu.n
                    mm0 = c_mm.n
                    ev0 = c_ev.n
                    ex0 = c_ex.n
                    xub3 = c_xu.n
                    tb0 = c_tbu.n
                    swlb = slot_bases(swl)
                    sxb3 = slot_bases(sx)
                    ssdb = slot_bases(ssd)

                    @block.gpsimd
                    def _(g):
                        for m in range(NCH):
                            if m >= 2:
                                g.wait_ge(c_mm.sem, mm0 + m - 1)
                            g.dma_start(
                                out=wslab[m % 2][:, :].rearrange(
                                    "p (c f) -> p c f", c=NCH),
                                in_=ffnWT[:, 128 * m:128 * (m + 1)].rearrange(
                                    "(c p) f -> p c f", p=128)
                            ).then_inc(swl[m % 2].sem, 16)
                    slot_bump(swl, 2, NCH)

                    def emit_ffn_stat(t, m):
                        t.wait_ge(c_ex.sem, ex0 + 2 * (m + 1))
                        if m == 0:
                            t.wait_ge(c_ev.sem, t_gvar)
                        for hf in range(2):
                            sl = slice(512 * hf, 512 * (hf + 1))
                            t.matmul(ps_s1[0:1, sl], ones_bc[:, :],
                                     tb[m % 2][:, sl], start=(m == 0),
                                     stop=(m == NCH - 1))
                            lasts = t.matmul(ps_s2[0:1, sl],
                                            ones_bc[:, :],
                                            tb[2 + m % 2][:, sl],
                                            start=(m == 0),
                                            stop=(m == NCH - 1))
                        lasts.then_inc(c_tbu.sem, 1)

                    @block.tensor
                    def _(t):
                        t.wait_ge(c_ev.sem, t_tmps)
                        for m in range(NCH):
                            t.wait_ge(*slot_tok(swl, swlb, 2, m))
                            if m >= 2:
                                t.wait_ge(c_ev.sem, ev0 + m - 1)
                            for kc in range(NCH):
                                for hf in range(2):
                                    t.matmul(
                                        ps_m[m % 2][:, 512 * hf:512 * (hf + 1)],
                                        wslab[m % 2][:, 128 * kc:128 * (kc + 1)],
                                        xfb[:, NT * kc + 512 * hf:
                                            NT * kc + 512 * (hf + 1)],
                                        start=(kc == 0), stop=False)
                            for hf in range(2):
                                last = t.matmul(
                                    ps_m[m % 2][:, 512 * hf:512 * (hf + 1)],
                                    lB_sb[:, 128 * m:128 * (m + 1)],
                                    tmps[:, 512 * hf:512 * (hf + 1)],
                                    start=False, stop=True)
                            last.then_inc(c_mm.sem, 1)
                            if next_stats and m >= 2:
                                emit_ffn_stat(t, m - 2)
                        if next_stats:
                            emit_ffn_stat(t, NCH - 2)
                            emit_ffn_stat(t, NCH - 1)
                    c_mm.n += NCH

                    @block.sync
                    def _(s):
                        for m in range(NCH + 2):
                            if m < NCH:
                                if m >= 3:
                                    s.wait_ge(c_ev.sem, ev0 + m - 2)
                                s.dma_start(out=xin[m % 3][:, :],
                                            in_=src[128 * m:128 * (m + 1), :]) \
                                    .then_inc(sx[m % 3].sem, 16)
                            if m >= 2:
                                s.wait_ge(c_ev.sem, ev0 + m - 1)
                                s.dma_start(
                                    out=dst[128 * (m - 2):128 * (m - 1), :],
                                    in_=xo_sb[m % 2][:, :]) \
                                    .then_inc(ssd[m % 2].sem, 16)
                    slot_bump(sx, 3, NCH)
                    slot_bump(ssd, 2, NCH)

                    @block.vector
                    def _(v):
                        for m in range(NCH):
                            v.wait_ge(c_mm.sem, mm0 + m + 1)
                            v.wait_ge(*slot_tok(sx, sxb3, 3, m))
                            if m >= 2:
                                v.wait_ge(*slot_tok(ssd, ssdb, 2, m - 2))
                                if next_stats:
                                    v.wait_ge(c_ex.sem, ex0 + 2 * (m - 1))
                            v.scalar_tensor_tensor(
                                xo_sb[m % 2][:, :], ps_m[m % 2][:, :], 0.5,
                                xin[m % 3][:, :], ALU.mult, ALU.add) \
                                .then_inc(c_ev.sem, 1)
                    c_ev.n += NCH



                    if next_stats:
                        @block.scalar
                        def _(sc):
                            for m in range(NCH):
                                sc.wait_ge(c_ev.sem, ev0 + m + 1)
                                if m >= 2:
                                    sc.wait_ge(c_tbu.sem, tb0 + m - 1)
                                sc.copy(tb[m % 2][:, :], xo_sb[m % 2][:, :]) \
                                    .then_inc(c_ex.sem, 1)
                                sc.square(tb[2 + m % 2][:, :],
                                          xo_sb[m % 2][:, :]) \
                                    .then_inc(c_ex.sem, 1)
                        c_ex.n += 2 * NCH

                        c_tbu.n += NCH
                        t_bc2, _ = ln_finalize(ps_s1, ps_s2,
                                               (c_tbu.sem, c_tbu.n), D,
                                               RO_LN, bmu[:, :],
                                               brstd[:, :], 128)
                    else:
                        t_bc2 = None
                return t_bc2

        t_bc2 = emit_ffn(xT, xres1, 0, True, "A")

        phase_fence()

        phB = contextlib.ExitStack()
        with phB:
            yb = phB.enter_context(nc.sbuf_tensor("ybB", [128, NCH * NT], BF16))
            xin = [phB.enter_context(nc.sbuf_tensor(f"xiB{i}", [128, NT], F32))
                   for i in range(3)]
            xot = phB.enter_context(nc.sbuf_tensor("xtB", [128, NT], F32))
            tbv = [phB.enter_context(nc.sbuf_tensor(f"tvB{i}", [128, NT], BF16))
                   for i in range(2)]
            yo = tbv
            tbq = [phB.enter_context(nc.sbuf_tensor(f"tqB{i}", [128, NT], BF16))
                   for i in range(2)]
            wslab = [phB.enter_context(
                nc.sbuf_tensor(f"wbB{i}", [128, NCH * 128], BF16))
                for i in range(2)]
            qki = [phB.enter_context(nc.sbuf_tensor(f"qiB{i}", [128, NT], BF16))
                   for i in range(2)]
            qko = [phB.enter_context(nc.sbuf_tensor(f"qoB{i}", [128, NT], BF16))
                   for i in range(2)]
            bmu2 = phB.enter_context(nc.sbuf_tensor("bmB", [128, NT], F32))
            brs2 = phB.enter_context(nc.sbuf_tensor("brB", [128, NT], F32))
            srow = phB.enter_context(nc.sbuf_tensor("srB", [1, 2048], F32))
            rwb = phB.enter_context(nc.sbuf_tensor("rwB", [1, 1024], F32))
            ps_q = [phB.enter_context(nc.psum_tensor(f"pBq{i}", [128, NT], F32))
                    for i in range(2)]
            ps_st = [phB.enter_context(nc.psum_tensor(f"pBs{i}", [1, 512], F32))
                     for i in range(4)]

            RAWT = {0: k_raw, 1: q_raw, 2: v_scr}
            APPT = {0: ka_scr, 1: qa_scr}
            WOFF = {0: 2, 1: 0}
            APL = {p: [] for p in range(12)}
            APL["tail"] = []
            for i in range(32):
                APL[6 + i // 8].append((0, i))
            for i in range(32):
                p = 10 + i // 8
                (APL[p] if p <= 11 else APL["tail"]).append((1, i))
            APBYSEQ = []
            for p in list(range(12)) + ["tail"]:
                APBYSEQ.extend(APL[p])
            APSEQ = {key: n for n, key in enumerate(APBYSEQ)}

            mm0 = c_mm.n
            MM = {(p, u): mm0 + 8 * p + u + 1
                  for p in range(12) for u in range(8)}
            c_mm.n += 96
            tb0s = c_tbu.n
            ST = {}
            _n = tb0s
            for p in range(8):
                for u in range(8):
                    _n += 1
                    ST[(p, u)] = _n
            c_tbu.n = _n

            SQ, FACT = {}, {}
            _n = c_ex.n
            for p in range(12):
                if p == 5:
                    FACT[0] = (_n + 1, _n + 2)
                    _n += 2
                if p == 9:
                    FACT[1] = (_n + 1, _n + 2)
                    _n += 2
                if p < 8:
                    for u in range(8):
                        _n += 1
                        SQ[(p, u)] = _n
            c_ex.n = _n

            EVY, EV, CPL, FDVE, APS, APM, APT = {}, {}, {}, {}, {}, {}, {}
            _n = c_ev.n
            for kc in range(NCH):
                EVY[kc] = (_n + 1, _n + 2)
                _n += 2
            for p in range(12):
                na = len(APL[p])
                for u in range(8):
                    _n += 1
                    EV[(p, u)] = _n
                    if u >= 2 and u - 2 < na:
                        key = APL[p][u - 2]
                        APS[key] = _n + 1
                        APM[key] = _n + 2
                        APT[key] = _n + 3
                        _n += 3
                for k_ in range(6, na):
                    key = APL[p][k_]
                    APS[key] = _n + 1
                    APM[key] = _n + 2
                    APT[key] = _n + 3
                    _n += 3
                if p < 8:
                    _n += 4
                    CPL[p] = _n
                if p == 4:
                    FDVE[0] = (_n + 1, _n + 12)
                    _n += 12
                if p == 9:
                    FDVE[1] = (_n + 1, _n + 12)
                    _n += 12
            for (tn, i) in APL["tail"]:
                APS[(tn, i)] = _n + 1
                APM[(tn, i)] = _n + 2
                APT[(tn, i)] = _n + 3
                _n += 3
            c_ev.n = _n

            sxb = slot_bases(sx)
            syab = slot_bases(sya)
            sybb = slot_bases(syb)
            swlb = slot_bases(swl)
            sstb = slot_bases(sst)
            stbb = slot_bases(stb)
            sqab = slot_bases(sqa)
            sarb = s_ar.n

            def SARSND(p):
                return sarb + 32 * (p + 1) + (16 if p >= 4 else 0)
            SAR_RK = sarb + 32 * 5 + 16
            SAR_RQ = sarb + 32 * 8 + 32

            def YST(kc):
                return sya[kc % 2].sem, syab[kc % 2] + 16 * (kc // 2 + 1)

            YGRP = []
            for a, n_ in enumerate([4, 4, 8, 8, 8]):
                for r in range(n_):
                    YGRP.append((a, r))
            YGO = [0, 4, 8, 16, 24]

            def YBL(p, kc):
                a = YGRP[kc][0]
                return syb[a].sem, sybb[a] + 16 * (p + 1)

            def SWL(ui):
                return swl[ui % 2].sem, swlb[ui % 2] + 16 * (ui // 2 + 1)

            def SSTK(ui):
                return sst[ui % 2].sem, sstb[ui % 2] + 16 * (ui // 2 + 1)

            def QKL(ki):
                return stb[ki % 2].sem, stbb[ki % 2] + 16 * (ki // 2 + 1)

            def QKS(ki):
                return sqa[ki % 2].sem, sqab[ki % 2] + 16 * (ki // 2 + 1)

            ybr0 = c_ybr.n
            c_ybr.n += 31 * 12
            cc0 = c_cc.n

            def T_AGY(a):
                return cc0 + a + 1
            T_ARK, T_ARQ = cc0 + 6, cc0 + 7
            c_cc.n += 7

            BNC = {}
            _b = s_bnc.n
            for tn in range(2):
                for s_ in range(4):
                    BNC[(tn, s_)] = _b + 64 * (4 * tn + s_ + 1)
            s_bnc.n = _b + 64 * 8

            def emit_apply(v, tn, i):
                s_, r_ = divmod(i, 8)
                ki = APSEQ[(tn, i)]
                v.wait_ge(*QKL(ki))
                v.wait_ge(s_bnc.sem, BNC[(tn, s_)])
                if ki >= 1:
                    v.wait_ge(c_ev.sem, APT[APBYSEQ[ki - 1]])
                else:
                    v.wait_ge(c_ev.sem, EVY[NCH - 1][1])
                v.tensor_sub(xot[:, :], qki[ki % 2][:, :],
                             bmu2[:, :]).then_inc(c_ev.sem, 1)
                v.wait_ge(c_ev.sem, APS[(tn, i)])
                v.tensor_mul(xot[:, :], xot[:, :],
                             brs2[:, :]).then_inc(c_ev.sem, 1)
                v.wait_ge(c_ev.sem, APM[(tn, i)])
                if ki >= 2:
                    v.wait_ge(*QKS(ki - 2))
                woff = WOFF[tn]
                v.tensor_scalar(qko[ki % 2][:, :], xot[:, :],
                                qkln_sb[:, 4 * r_ + woff:4 * r_ + woff + 1],
                                qkln_sb[:, 4 * r_ + woff + 1:
                                        4 * r_ + woff + 2],
                                ALU.mult, ALU.add).then_inc(c_ev.sem, 1)

            def emit_finalize_dve(v, tn):
                v.wait_ge(s_ar.sem, SAR_RK if tn == 0 else SAR_RQ)
                base = FDVE[tn][0] - 1
                for s_ in range(4):
                    mu = rows[0:1, 1024 * s_:1024 * (s_ + 1)]
                    s2 = rows[0:1, 4096 + 1024 * s_:4096 + 1024 * (s_ + 1)]
                    if s_ >= 1:
                        v.wait_ge(c_ev.sem, base + 3 * s_)
                    v.tensor_scalar_mul(mu, mu, 1.0 / D).then_inc(c_ev.sem, 1)
                    v.wait_ge(c_ev.sem, base + 3 * s_ + 1)
                    v.tensor_mul(rwb[0:1, :], mu, mu).then_inc(c_ev.sem, 1)
                    v.wait_ge(c_ev.sem, base + 3 * s_ + 2)
                    v.scalar_tensor_tensor(s2, s2, 1.0 / D, rwb[0:1, :],
                                           ALU.mult, ALU.subtract) \
                        .then_inc(c_ev.sem, 1)

            @block.vector
            def _(v):
                v.wait_ge(s_lnr.sem, t_bc2)
                for kc in range(NCH):
                    v.wait_ge(*slot_tok(sx, sxb, 3, kc))
                    if kc >= 1:
                        v.wait_ge(c_ev.sem, EVY[kc - 1][1])
                    v.tensor_sub(xot[:, :], xin[kc % 3][:, :],
                                 bmu[:, :]).then_inc(c_ev.sem, 1)
                    if kc >= 2:
                        v.wait_ge(*YST(kc - 2))
                    v.wait_ge(c_ev.sem, EVY[kc][0])
                    v.tensor_mul(yo[kc % 2][:, :], xot[:, :],
                                 brstd[:, :]).then_inc(c_ev.sem, 1)

            for p in range(12):
                @block.vector
                def _(v, p=p):
                    na = len(APL[p])
                    for u in range(8):
                        ui = 8 * p + u
                        v.wait_ge(c_mm.sem, MM[(p, u)])
                        if ui < 2:
                            v.wait_ge(*YST(NCH - 2 + ui))
                        if ui >= 2:
                            v.wait_ge(*SSTK(ui - 2))
                            pp, uu = divmod(ui - 2, 8)
                            if (pp, uu) in SQ:
                                v.wait_ge(c_ex.sem, SQ[(pp, uu)])
                            if (pp, uu) in ST:
                                v.wait_ge(c_tbu.sem, ST[(pp, uu)])
                        v.tensor_scalar(tbv[ui % 2][:, :], ps_q[ui % 2][:, :],
                                        8.0, -8.0, ALU.min, ALU.max) \
                            .then_inc(c_ev.sem, 1)
                        if u >= 2 and u - 2 < na:
                            tn, i = APL[p][u - 2]
                            emit_apply(v, tn, i)
                    for k_ in range(6, na):
                        tn, i = APL[p][k_]
                        emit_apply(v, tn, i)
                    if p < 8:
                        v.wait_ge(c_tbu.sem, ST[(p, 7)])
                        if p >= 1:
                            v.wait_ge(s_ar.sem, SARSND(p - 1))
                        for q4 in range(4):
                            dst = 512 * (q4 % 2) + 1024 * (q4 // 2)
                            v.tensor_copy(srow[0:1, dst:dst + 512],
                                          ps_st[q4][0:1, :]) \
                                .then_inc(c_ev.sem, 1)
                    if p == 4:
                        emit_finalize_dve(v, 0)
                    if p == 9:
                        emit_finalize_dve(v, 1)

            @block.vector
            def _(v):
                for (tn, i) in APL["tail"]:
                    emit_apply(v, tn, i)

            @block.scalar
            def _(sc):
                for p in range(12):
                    if p in (5, 9):
                        tn = 0 if p == 5 else 1
                        sc.wait_ge(c_ev.sem, FDVE[tn][1])
                        sc.activation(rows[0:1, 4096:8192],
                                      rows[0:1, 4096:8192], AF.Ln,
                                      bias=epsc[0:1, 0:1]).then_inc(c_ex.sem, 1)
                        sc.wait_ge(c_ex.sem, FACT[tn][0])
                        sc.activation(rows[0:1, 4096:8192],
                                      rows[0:1, 4096:8192], AF.Exp,
                                      scale=-0.5).then_inc(c_ex.sem, 1)
                    if p < 8:
                        for u in range(8):
                            ui = 8 * p + u
                            sc.wait_ge(c_ev.sem, EV[(p, u)])
                            if ui >= 2:
                                pp, uu = divmod(ui - 2, 8)
                                if (pp, uu) in ST:
                                    sc.wait_ge(c_tbu.sem, ST[(pp, uu)])
                            sc.square(tbq[ui % 2][:, :], tbv[ui % 2][:, :]) \
                                .then_inc(c_ex.sem, 1)

            def emit_stat(t, p, u):
                t.wait_ge(c_ex.sem, SQ[(p, u)])
                if u == 0 and p >= 1:
                    t.wait_ge(c_ev.sem, CPL[p - 1])
                ui = 8 * p + u
                for q4 in range(4):
                    src = tbv if q4 < 2 else tbq
                    last = t.matmul(
                        ps_st[q4][0:1, :], ones_bc[:, :],
                        src[ui % 2][:, 512 * (q4 % 2):512 * (q4 % 2 + 1)],
                        start=(u == 0), stop=(u == 7))
                last.then_inc(c_tbu.sem, 1)

            @block.tensor
            def _(t):
                for p in range(12):
                    third = p // 4
                    for u in range(8):
                        ui = 8 * p + u
                        m = 8 * third + u
                        t.wait_ge(*SWL(ui))
                        if ui >= 2:
                            t.wait_ge(c_ev.sem, EV[divmod(ui - 2, 8)])
                        for kc in range(NCH):
                            if u == 0 and kc in YGO:
                                t.wait_ge(*YBL(p, kc))
                            for hf in range(2):
                                last = t.matmul(
                                    ps_q[ui % 2][:, 512 * hf:512 * (hf + 1)],
                                    wslab[ui % 2][:, 128 * kc:128 * (kc + 1)],
                                    yb[:, NT * kc + 512 * hf:
                                       NT * kc + 512 * (hf + 1)],
                                    start=(kc == 0), stop=(kc == NCH - 1))
                            if u == 7 and kc < NCH - 1:
                                last.then_inc(c_ybr.sem, 1)
                        last.then_inc(c_mm.sem, 1)
                        if p < 8 and u >= 2:
                            emit_stat(t, p, u - 2)
                    if p < 8:
                        emit_stat(t, p, 6)
                        emit_stat(t, p, 7)

            @block.sync
            def _(s):
                for kc in range(NCH + 2):
                    if kc < NCH:
                        if kc >= 3:
                            s.wait_ge(c_ev.sem, EVY[kc - 3][0])
                        s.dma_start(out=xin[kc % 3][:, :],
                                    in_=xres1[128 * kc:128 * (kc + 1), :]) \
                            .then_inc(sx[kc % 3].sem, 16)
                    if kc >= 2:
                        kp = kc - 2
                        ga, gr = YGRP[kp]
                        s.wait_ge(c_ev.sem, EVY[kp][1])
                        s.dma_start(
                            out=agy_i[ga][128 * gr:128 * (gr + 1), :],
                            in_=yo[kp % 2][:, :]).then_inc(sya[kp % 2].sem, 16)
            slot_bump(sx, 3, NCH)
            slot_bump(sya, 2, NCH)

            for p in range(12):
                @block.sync
                def _(s, p=p):
                    third, tb = p // 4, p % 4
                    _yg = [4, 4, 8, 8, 8]
                    for a in range(5):
                        n_ = _yg[a]
                        lastc = YGO[a] + n_ - 1
                        if p >= 1:
                            if lastc < NCH - 1:
                                s.wait_ge(c_ybr.sem,
                                          ybr0 + 31 * (p - 1) + lastc + 1)
                            else:
                                s.wait_ge(c_mm.sem, MM[(p - 1, 7)])
                        s.wait_ge(c_cc.sem, T_AGY(a))
                        s.dma_start(
                            out=yb[:, NT * YGO[a]:NT * (YGO[a] + n_)]
                            .rearrange("p (r t) -> p r t", r=n_),
                            in_=agy_o[a][128 * n_ * tb:128 * n_ * (tb + 1), :]
                            .rearrange("(r p) t -> p r t", p=128)) \
                            .then_inc(syb[a].sem, 16)
                    for u in range(8):
                        ui = 8 * p + u
                        s.wait_ge(c_ev.sem, EV[(p, u)])
                        if p < 8:
                            s.wait_ge(c_ex.sem, SQ[(p, u)])
                        s.dma_start(
                            out=RAWT[third][128 * u:128 * (u + 1),
                                            1024 * tb:1024 * (tb + 1)],
                            in_=tbv[ui % 2][:, :]).then_inc(sst[ui % 2].sem, 16)
                    if p < 8:
                        tn, tbs = p // 4, p % 4
                        dst = arki if tn == 0 else arqi
                        s.wait_ge(c_ev.sem, CPL[p])
                        s.dma_start(out=dst[0:1, 1024 * tbs:1024 * (tbs + 1)],
                                    in_=srow[0:1, 0:1024]) \
                            .then_inc(s_ar.sem, 16)
                        s.dma_start(out=dst[0:1, 4096 + 1024 * tbs:
                                            4096 + 1024 * (tbs + 1)],
                                    in_=srow[0:1, 1024:2048]) \
                            .then_inc(s_ar.sem, 16)
                    if p in (4, 9):
                        tn = 0 if p == 4 else 1
                        s.wait_ge(c_cc.sem, T_ARK if tn == 0 else T_ARQ)
                        if tn == 0:
                            s.wait_ge(s_lnr.sem, t_bc2)
                        else:
                            s.wait_ge(s_bnc.sem, BNC[(0, 3)])
                        s.dma_start(out=rows[0:1, :],
                                    in_=(arko if tn == 0 else arqo)) \
                            .then_inc(s_ar.sem, 16)
            for _p in range(12):
                for _a in range(5):
                    syb[_a].n += 16
            slot_bump(sst, 2, 96)
            slot_bump(stb, 2, 64)
            slot_bump(sqa, 2, 64)
            s_ar.n += 16 * 18

            def emit_apply_load(g, tn, i):
                s_i, r_ = divmod(i, 8)
                ki = APSEQ[(tn, i)]
                if ki >= 2:
                    g.wait_ge(c_ev.sem, APS[APBYSEQ[ki - 2]])
                rawp = 4 * tn + s_i
                g.wait_ge(*SSTK(8 * rawp + r_))
                g.dma_start(out=qki[ki % 2][:, :],
                            in_=RAWT[tn][128 * r_:128 * (r_ + 1),
                                         1024 * s_i:1024 * (s_i + 1)]) \
                    .then_inc(stb[ki % 2].sem, 16)

            def emit_apply_store(g, tn, i):
                s_i, r_ = divmod(i, 8)
                ki = APSEQ[(tn, i)]
                g.wait_ge(c_ev.sem, APT[(tn, i)])
                g.dma_start(out=APPT[tn][128 * r_:128 * (r_ + 1),
                                         1024 * s_i:1024 * (s_i + 1)],
                            in_=qko[ki % 2][:, :]).then_inc(sqa[ki % 2].sem, 16)

            def emit_bounce(g, tn, s_):
                base = BNC[(tn, s_)] - 64
                g.wait_ge(s_bnc.sem, base)
                if tn == 0 and s_ == 0:
                    g.wait_ge(s_lnr.sem, t_bc2)
                g.wait_ge(c_ev.sem, FDVE[tn][0])
                g.wait_ge(c_ex.sem, FACT[tn][1])
                idx = 4 * tn + s_
                if idx >= 1:
                    tp, sp = divmod(idx - 1, 4)
                    g.wait_ge(c_ev.sem, APT[(tp, 8 * sp + 7)])
                g.dma_start(out=rsc[0:1, RO_LN:RO_LN + NT],
                            in_=rows[0:1, 1024 * s_:1024 * (s_ + 1)]) \
                    .then_inc(s_bnc.sem, 16)
                g.dma_start(out=rsc[0:1, RO_LN + NT:RO_LN + 2 * NT],
                            in_=rows[0:1, 4096 + 1024 * s_:
                                     4096 + 1024 * (s_ + 1)]) \
                    .then_inc(s_bnc.sem, 16)
                g.wait_ge(s_bnc.sem, base + 32)
                g.dma_start(out=bmu2[:, :],
                            in_=rsc[0:1, RO_LN:RO_LN + NT]
                            .partition_broadcast(128)).then_inc(s_bnc.sem, 16)
                g.dma_start(out=brs2[:, :],
                            in_=rsc[0:1, RO_LN + NT:RO_LN + 2 * NT]
                            .partition_broadcast(128)).then_inc(s_bnc.sem, 16)

            @block.gpsimd
            def _(g):
                _cum = [4, 8, 16, 24, 32]
                for a in range(5):
                    need = 16 * (_cum[a] // 2)
                    g.wait_ge(sya[0].sem, syab[0] + need)
                    g.wait_ge(sya[1].sem, syab[1] + need)
                    g.collective_compute(
                        "AllGather", ALU.bypass,
                        replica_groups=[[0, 1, 2, 3], [4, 5, 6, 7]],
                        ins=[agy_i[a].opt()], outs=[agy_o[a].opt()]) \
                        .then_inc(c_cc.sem, 1)
                for p in range(12):
                    third = p // 4
                    na = len(APL[p])
                    for u in range(8):
                        ui = 8 * p + u
                        m = 8 * third + u
                        if ui >= 2:
                            g.wait_ge(c_mm.sem, MM[divmod(ui - 2, 8)])
                        g.dma_start(
                            out=wslab[ui % 2][:, :].rearrange(
                                "p (c f) -> p c f", c=NCH),
                            in_=wqkvT[:, 128 * m:128 * (m + 1)].rearrange(
                                "(c p) f -> p c f", p=128)
                        ).then_inc(swl[ui % 2].sem, 16)
                        if u >= 2 and u - 2 < na:
                            tn, i = APL[p][u - 2]
                            emit_apply_load(g, tn, i)
                            if u - 4 >= 0 and u - 4 < na:
                                emit_apply_store(g, *APL[p][u - 4])
                    for k_ in range(6, na):
                        tn, i = APL[p][k_]
                        emit_apply_load(g, tn, i)
                        emit_apply_store(g, *APL[p][k_ - 2])
                    if p in (3, 7):
                        tn = 0 if p == 3 else 1
                        g.wait_ge(s_ar.sem, SARSND(p))
                        g.collective_compute(
                            "AllReduce", ALU.add,
                            replica_groups=[[0, 1, 2, 3], [4, 5, 6, 7]],
                            ins=[(arki if tn == 0 else arqi).opt()],
                            outs=[(arko if tn == 0 else arqo).opt()]) \
                            .then_inc(c_cc.sem, 1)
                    if 5 <= p <= 8:
                        emit_bounce(g, 0, p - 5)
                    if 9 <= p <= 11:
                        emit_bounce(g, 1, p - 9)
                    for k_ in range(max(0, na - 2), na):
                        emit_apply_store(g, *APL[p][k_])

            @block.gpsimd
            def _(g):
                for (tn, i) in APL["tail"][:8]:
                    emit_apply_load(g, tn, i)
                    emit_apply_store(g, tn, i)
                emit_bounce(g, 1, 3)
                for (tn, i) in APL["tail"][8:]:
                    emit_apply_load(g, tn, i)
                    emit_apply_store(g, tn, i)
            slot_bump(swl, 2, 96)

        phase_fence()

        phC = contextlib.ExitStack()
        with phC:
            kh = [phC.enter_context(nc.sbuf_tensor(f"kh{i}", [128, S], BF16))
                  for i in range(2)]
            qh = [phC.enter_context(nc.sbuf_tensor(f"qh{i}", [128, S], BF16))
                  for i in range(2)]
            vth = phC.enter_context(nc.sbuf_tensor("vth", [128, S], BF16))
            vh = [phC.enter_context(nc.sbuf_tensor(f"vh{i}", [128, S], BF16))
                  for i in range(2)]
            nsq = phC.enter_context(nc.sbuf_tensor("nsq", [128, S], BF16))
            crb = phC.enter_context(nc.sbuf_tensor("crb", [1, S], BF16))
            aq_sb = phC.enter_context(nc.sbuf_tensor("aq_sb", [1, S], F32))
            ptl = [phC.enter_context(nc.sbuf_tensor(f"ptl{i}", [128, 512], BF16))
                   for i in range(8)]
            dbb = [phC.enter_context(nc.sbuf_tensor(f"dbb{i}", [128, 512], F32))
                   for i in range(2)]
            obf = [phC.enter_context(nc.sbuf_tensor(f"obf{i}", [128, 512], BF16))
                   for i in range(2)]
            sca = phC.enter_context(nc.sbuf_tensor("sca", [1, 8], F32))
            ps_sc = [phC.enter_context(nc.psum_tensor(f"pCs{i}", [128, 512], F32))
                     for i in range(3)]
            ps_o = [phC.enter_context(nc.psum_tensor(f"pCo{i}", [128, 512], F32))
                    for i in range(2)]
            ps_vt = phC.enter_context(nc.psum_tensor("pCv", [128, 512], BF16))
            ps_dn = [phC.enter_context(nc.psum_tensor(f"pCd{i}", [1, 512], F32))
                     for i in range(2)]
            ps_nm = ps_dn[0]

            tiles = []
            for hh in range(HPC):
                for qb in range(8):
                    for kc in range(4 * qb + 4):
                        tiles.append((hh, qb, kc))
            NTL = len(tiles)
            last_ti_of_head = {}
            last_ti_of_g = {}
            for ti, (hh, qb, kc) in enumerate(tiles):
                last_ti_of_head[hh] = ti
                last_ti_of_g[8 * hh + qb] = ti

            def qs_of(qb, kc):
                return max(0, 128 * kc - 512 * qb)

            cmm0, cex0, cev0 = c_mm.n, c_ev.n, c_ex.n
            cev0, cex0 = c_ev.n, c_ex.n
            amm0, asu0, aex0 = c_amm.n, c_asu.n, c_aex.n
            apv0, aac0, aon0, adn0 = c_apv.n, c_aac.n, c_aon.n, c_adn.n
            skhb, sqhb = slot_bases(skh), slot_bases(sqh)
            svtb = svt.n
            sobb, sdvb, sdbb = slot_bases(sob), slot_bases(sdv), slot_bases(sdb)
            aqb, cpbb, cbbb = s_aq.n, s_cpb.n, s_cbb.n

            def KH_TOK(h):
                return skh[h % 2].sem, skhb[h % 2] + 16 * (h // 2 + 1)

            def QH_TOK(h):
                return sqh[h % 2].sem, sqhb[h % 2] + 16 * (h // 2 + 1)

            def VT_TOK(h):
                return svt.sem, svtb + 16 * (h + 1)

            def SOB_TOK(g):
                return sob[g % 2].sem, sobb[g % 2] + 16 * (g // 2 + 1)

            def SDV_TOK(g):
                return sdv[g % 2].sem, sdvb[g % 2] + 16 * (g // 2 + 1)

            def SDB_TOK(g):
                return sdb[g % 2].sem, sdbb[g % 2] + 16 * (g // 2 + 1)

            def TK(h, b):
                return cmm0 + 24 * h + b + 1

            def TQ(h, b):
                return cmm0 + 24 * h + 8 + b + 1

            def TV(h, g):
                return cmm0 + 24 * h + 16 + g + 1

            def SQK(h):
                return cex0 + 12 * h + 1

            def SQQ(h):
                return cex0 + 12 * h + 2

            def SQM(h):
                return cex0 + 12 * h + 3

            def SQN(h):
                return cex0 + 12 * h + 4

            def VEV(h, g):
                return cex0 + 12 * h + 5 + g

            def KEV(h, b):
                return cev0 + 36 * h + b + 1

            def KMAX(h):
                return cev0 + 36 * h + 9

            def QEV(h, b):
                return cev0 + 36 * h + 10 + b

            def KSCL(h):
                return cev0 + 36 * h + 18

            def CPP(h):
                return cev0 + 36 * h + 19

            def CRB(h):
                return cev0 + 36 * h + 20

            def CLAMP(h, qb):
                return cev0 + 36 * h + 20 + 2 * qb + 1

            def RECIP(h, qb):
                return cev0 + 36 * h + 20 + 2 * qb + 2

            def AQL(h):
                return aqb + 16 * (h + 1)

            def CPB(h):
                return cpbb + 16 * (h + 1)

            def CBB(h):
                return cbbb + 16 * (h + 1)


            def emit_kq_loads(g, h):
                if h >= 2:
                    g.wait_ge(c_amm.sem, amm0 + last_ti_of_head[h - 2] + 1)
                g.dma_start(out=kh[h % 2][:, :],
                            in_=ka_scr[128 * h:128 * (h + 1), :]) \
                    .then_inc(skh[h % 2].sem, 16)
                g.dma_start(out=qh[h % 2][:, :],
                            in_=qa_scr[128 * h:128 * (h + 1), :]) \
                    .then_inc(sqh[h % 2].sem, 16)

            def emit_v_loads(g, h):
                if h >= 1:
                    g.wait_ge(c_mm.sem, TV(h - 1, 7))
                g.dma_start(out=vth[:, :],
                            in_=v_scr[128 * h:128 * (h + 1), :]) \
                    .then_inc(svt.sem, 16)

            @block.gpsimd
            def _(g):
                emit_kq_loads(g, 0)
                emit_v_loads(g, 0)
                for h in range(1, HPC):
                    emit_kq_loads(g, h)
                    emit_v_loads(g, h)
            for h in range(HPC):
                skh[h % 2].n += 16
                sqh[h % 2].n += 16
            svt.n += 16 * HPC

            @block.gpsimd
            def _(g):
                g.wait_ge(sob[0].sem, sobb[0] + 16 * 28)
                g.wait_ge(sob[1].sem, sobb[1] + 16 * 28)
                g.collective_compute(
                    "AllGather", ALU.bypass,
                    replica_groups=[[0, 1, 2, 3], [4, 5, 6, 7]],
                    ins=[aoiA.opt()], outs=[aooA.opt()]).then_inc(c_cc.sem, 1)
                g.wait_ge(sob[0].sem, sobb[0] + 16 * 32)
                g.wait_ge(sob[1].sem, sobb[1] + 16 * 32)
                g.collective_compute(
                    "AllGather", ALU.bypass,
                    replica_groups=[[0, 1, 2, 3], [4, 5, 6, 7]],
                    ins=[aoiB.opt()], outs=[aooB.opt()]).then_inc(c_cc.sem, 1)
            c_cc.n += 2
            T_AGO1, T_AGO2 = c_cc.n - 1, c_cc.n

            def emit_pv(t, tj):
                hj, qj, kj = tiles[tj]
                gj = 8 * hj + qj
                qsj = qs_of(qj, kj)
                t.wait_ge(c_aex.sem, aex0 + tj + 1)
                t.wait_ge(c_ex.sem, VEV(hj, kj // 4))
                if kj == 0 and gj >= 2:
                    t.wait_ge(c_aon.sem, aon0 + gj - 1)
                t.matmul(ps_o[gj % 2][:, qsj:512],
                         vh[hj % 2][:, 128 * kj:128 * (kj + 1)],
                         ptl[tj % 8][:, qsj:512],
                         start=(kj == 0), stop=(kj == 4 * qj + 3)) \
                    .then_inc(c_apv.sem, 1)
                if kj == 0:
                    if gj >= 2:
                        h2, q2 = divmod(gj - 2, 8)
                        t.wait_ge(c_ev.sem, CLAMP(h2, q2))
                    if qj == 0:
                        t.wait_ge(c_ev.sem, QEV(hj, 7))
                t.matmul(ps_dn[qj % 2][0:1, qsj:512], ones_bc[:, :],
                         ptl[tj % 8][:, qsj:512],
                         start=(kj == 0), stop=(kj == 4 * qj + 3)) \
                    .then_inc(c_adn.sem, 1)

            @block.tensor
            def _(t):
                pend_pv = []
                for ti, (hh, qb, kc) in enumerate(tiles):
                    g = 8 * hh + qb
                    if qb == 0 and kc == 0:
                        while pend_pv:
                            emit_pv(t, pend_pv.pop(0))
                    if qb == 0 and kc == 0:
                        t.wait_ge(c_ex.sem, SQK(hh))
                        for b in range(8):
                            if b == 0 and hh >= 1:
                                t.wait_ge(c_ev.sem, QEV(hh - 1, 7))
                                t.wait_ge(c_ev.sem, CLAMP(hh - 1, 6))
                            if b >= 1:
                                t.wait_ge(c_ev.sem, KEV(hh, b - 1))
                            t.matmul(ps_nm[0:1, :], ones_bc[:, :],
                                     nsq[:, 512 * b:512 * (b + 1)],
                                     start=True, stop=True).then_inc(c_mm.sem, 1)
                        t.wait_ge(c_ex.sem, SQQ(hh))
                        for b in range(8):
                            t.wait_ge(c_ev.sem,
                                      KEV(hh, 7) if b == 0 else QEV(hh, b - 1))
                            t.matmul(ps_nm[0:1, :], ones_bc[:, :],
                                     nsq[:, 512 * b:512 * (b + 1)],
                                     start=True, stop=True).then_inc(c_mm.sem, 1)
                        t.wait_ge(*VT_TOK(hh))
                        for grp in range(8):
                            t.wait_ge(c_ex.sem,
                                      VEV(hh, grp - 1) if grp else
                                      (VEV(hh - 1, 7) if hh else 0))
                            for i4 in range(4):
                                kk = 4 * grp + i4
                                last = t.transpose(
                                    ps_vt[:, 128 * i4:128 * (i4 + 1)],
                                    vth[:, 128 * kk:128 * (kk + 1)],
                                    ident[:, :])
                            last.then_inc(c_mm.sem, 1)
                    if qb == 0 and kc == 0:
                        t.wait_ge(*KH_TOK(hh))
                        t.wait_ge(*QH_TOK(hh))
                        t.wait_ge(c_ev.sem, CRB(hh))
                    if ti >= 3:
                        t.wait_ge(c_aex.sem, aex0 + ti - 2)
                    qs = qs_of(qb, kc)
                    t.matmul(ps_sc[ti % 3][:, qs:512],
                             kh[hh % 2][:, 128 * kc:128 * (kc + 1)],
                             qh[hh % 2][:, 512 * qb + qs:512 * (qb + 1)],
                             start=True, stop=False)
                    t.matmul(ps_sc[ti % 3][:, qs:512], ones_r[0:1, :],
                             crb[0:1, 512 * qb + qs:512 * (qb + 1)],
                             start=False, stop=True).then_inc(c_amm.sem, 1)
                    pend_pv.append(ti)
                    if len(pend_pv) > 2:
                        emit_pv(t, pend_pv.pop(0))
                for tj in pend_pv:
                    emit_pv(t, tj)
            c_mm.n += 24 * HPC
            c_amm.n += NTL
            c_apv.n += NTL
            c_adn.n += NTL

            msk0 = c_msk.n
            mcount = []
            _mc = 0
            for (hh, qb, kc) in tiles:
                if kc >= 4 * qb:
                    _mc += 1
                mcount.append(_mc)

            @block.scalar
            def _(sc):
                for ti, (hh, qb, kc) in enumerate(tiles):
                    if qb == 0 and kc == 0:
                        sc.wait_ge(*KH_TOK(hh))
                        if hh >= 1:
                            sc.wait_ge(c_mm.sem, TQ(hh - 1, 7))
                        sc.square(nsq[:, :], kh[hh % 2][:, :]) \
                            .then_inc(c_ex.sem, 1)
                        sc.wait_ge(c_mm.sem, TK(hh, 7))
                        sc.wait_ge(*QH_TOK(hh))
                        sc.square(nsq[:, :], qh[hh % 2][:, :]) \
                            .then_inc(c_ex.sem, 1)
                        sc.wait_ge(c_ev.sem, KMAX(hh))
                        sc.activation(sca[0:1, 1:2], sca[0:1, 0:1],
                                      AF.Sqrt).then_inc(c_ex.sem, 1)
                        sc.wait_ge(c_ev.sem, QEV(hh, 7))
                        sc.activation(rows[0:1, 4096:8192],
                                      rows[0:1, 4096:8192], AF.Sqrt) \
                            .then_inc(c_ex.sem, 1)
                        for grp in range(8):
                            sc.wait_ge(c_mm.sem, TV(hh, grp))
                            if grp == 0 and hh >= 2:
                                sc.wait_ge(
                                    c_apv.sem,
                                    apv0 + last_ti_of_head[hh - 2] + 1)
                            sc.copy(vh[hh % 2][:, 512 * grp:512 * (grp + 1)],
                                    ps_vt[:, :]).then_inc(c_ex.sem, 1)
                    qs = qs_of(qb, kc)
                    if kc >= 4 * qb:
                        sc.wait_ge(c_msk.sem, msk0 + mcount[ti])
                    else:
                        sc.wait_ge(c_amm.sem, amm0 + ti + 1)
                    if ti >= 8:
                        sc.wait_ge(c_apv.sem, apv0 + ti - 7)
                    sc.activation(ptl[ti % 8][:, qs:512],
                                  ps_sc[ti % 3][:, qs:512], AF.Exp,
                                  bias=alibik_sb[:, 32 * hh + kc:
                                                 32 * hh + kc + 1],
                                  scale=SCALE).then_inc(c_aex.sem, 1)
            c_ex.n += 12 * HPC
            c_aex.n += NTL

            @block.vector
            def _(v):
                pend_on = []
                for ti, (hh, qb, kc) in enumerate(tiles):
                    g = 8 * hh + qb
                    if qb == 0 and kc == 0:
                        for b in range(8):
                            v.wait_ge(c_mm.sem, TK(hh, b))
                            if b == 0 and hh >= 1:
                                v.wait_ge(*SDV_TOK(8 * (hh - 1) + 7))
                                v.wait_ge(*SDV_TOK(8 * (hh - 1) + 6))
                                v.wait_ge(c_ex.sem, SQN(hh - 1))
                            if b == 0 and hh == 0:
                                v.wait_ge(s_lnr.sem, s_lnr.n)
                            v.tensor_copy(rows[0:1, 512 * b:512 * (b + 1)],
                                          ps_nm[0:1, :]).then_inc(c_ev.sem, 1)
                        v.wait_ge(c_ev.sem, KEV(hh, 7))
                        v.reduce_max(sca[0:1, 0:1], rows[0:1, 0:4096],
                                     axis=AX).then_inc(c_ev.sem, 1)
                        for b in range(8):
                            v.wait_ge(c_mm.sem, TQ(hh, b))
                            v.tensor_copy(
                                rows[0:1, 4096 + 512 * b:4096 + 512 * (b + 1)],
                                ps_nm[0:1, :]).then_inc(c_ev.sem, 1)
                        v.wait_ge(c_ex.sem, SQM(hh))
                        v.tensor_scalar_mul(sca[0:1, 2:3], sca[0:1, 1:2],
                                            1.01).then_inc(c_ev.sem, 1)
                        v.wait_ge(c_ex.sem, SQN(hh))
                        v.wait_ge(c_ev.sem, KSCL(hh))
                        v.wait_ge(s_aq.sem, AQL(hh))
                        v.scalar_tensor_tensor(
                            rows[0:1, 0:4096], rows[0:1, 4096:8192],
                            sca[0:1, 2:3], aq_sb[0:1, :],
                            ALU.mult, ALU.add).then_inc(c_ev.sem, 1)
                        v.wait_ge(c_ev.sem, CPP(hh))
                        if hh >= 1:
                            v.wait_ge(c_amm.sem,
                                      amm0 + last_ti_of_head[hh - 1] + 1)
                        v.tensor_scalar_mul(crb[0:1, :], rows[0:1, 0:4096],
                                            -1.0).then_inc(c_ev.sem, 1)
                    qs = qs_of(qb, kc)
                    if kc >= 4 * qb:
                        v.wait_ge(c_amm.sem, amm0 + ti + 1)
                        v.tensor_add(ps_sc[ti % 3][:, qs:qs + 128],
                                     ps_sc[ti % 3][:, qs:qs + 128],
                                     mtri[:, :]).then_inc(c_msk.sem, 1)
                    if kc == 4 * qb + 3:
                        v.wait_ge(c_adn.sem, adn0 + last_ti_of_g[g] + 1)
                        dvslot = 512 * (qb % 2)
                        if g >= 2:
                            v.wait_ge(*SDV_TOK(g - 2))
                        v.tensor_scalar_max(rows[0:1, dvslot:dvslot + 512],
                                            ps_dn[qb % 2][0:1, :], 1e-30) \
                            .then_inc(c_ev.sem, 1)
                        v.wait_ge(c_ev.sem, CLAMP(hh, qb))
                        v.reciprocal(rows[0:1, dvslot:dvslot + 512],
                                     rows[0:1, dvslot:dvslot + 512]) \
                            .then_inc(c_ev.sem, 1)
                        pend_on.append(g)
                        if len(pend_on) > 1:
                            gp = pend_on.pop(0)
                            hp, qp = divmod(gp, 8)
                            v.wait_ge(c_apv.sem,
                                      apv0 + last_ti_of_g[gp] + 1)
                            v.wait_ge(*SDB_TOK(gp))
                            if gp >= 2:
                                v.wait_ge(*SOB_TOK(gp - 2))
                            v.tensor_mul(obf[gp % 2][:, :],
                                         ps_o[gp % 2][:, :],
                                         dbb[gp % 2][:, :]) \
                                .then_inc(c_aon.sem, 1)
                for gp in pend_on:
                    hp, qp = divmod(gp, 8)
                    v.wait_ge(c_apv.sem, apv0 + last_ti_of_g[gp] + 1)
                    v.wait_ge(*SDB_TOK(gp))
                    v.wait_ge(*SOB_TOK(gp - 2))
                    v.tensor_mul(obf[gp % 2][:, :], ps_o[gp % 2][:, :],
                                 dbb[gp % 2][:, :]).then_inc(c_aon.sem, 1)
            c_ev.n += 36 * HPC
            c_msk.n += mcount[-1]
            c_aon.n += 64

            @block.sync
            def _(s):
                for hh in range(HPC):
                    if hh >= 1:
                        s.wait_ge(c_ev.sem, CPP(hh - 1))
                    s.dma_start(out=aq_sb[0:1, :],
                                in_=alibiq[0:1, S * hh:S * (hh + 1)]) \
                        .then_inc(s_aq.sem, 16)
                    for qb in range(8):
                        g = 8 * hh + qb
                        dvslot = 512 * (qb % 2)
                        s.wait_ge(c_ev.sem, RECIP(hh, qb))
                        if g >= 2:
                            s.wait_ge(*SDB_TOK(g - 2))
                        s.dma_start(
                            out=rsc[0:1, RO_DNV + 512 * (g % 2):
                                    RO_DNV + 512 * (g % 2) + 512],
                            in_=rows[0:1, dvslot:dvslot + 512]) \
                            .then_inc(sdv[g % 2].sem, 16)
                        s.wait_ge(*SDV_TOK(g))
                        if g >= 2:
                            s.wait_ge(c_aon.sem, aon0 + g - 1)
                        s.dma_start(
                            out=dbb[g % 2][:, :],
                            in_=rsc[0:1, RO_DNV + 512 * (g % 2):
                                    RO_DNV + 512 * (g % 2) + 512]
                            .partition_broadcast(128)).then_inc(sdb[g % 2].sem, 16)
                        if g >= 1:
                            hp, qp = divmod(g - 1, 8)
                            s.wait_ge(c_aon.sem, aon0 + g)
                            dstO = (aoiA[128 * hp:128 * (hp + 1),
                                          512 * qp:512 * (qp + 1)]
                                    if hp < 7 else
                                    aoiB[0:128, 512 * qp:512 * (qp + 1)])
                            s.dma_start(out=dstO,
                                        in_=obf[(g - 1) % 2][:, :]) \
                                .then_inc(sob[(g - 1) % 2].sem, 16)
                s.wait_ge(c_aon.sem, aon0 + 64)
                s.dma_start(out=aoiB[0:128, 512 * 7:512 * 8],
                            in_=obf[63 % 2][:, :]).then_inc(sob[63 % 2].sem, 16)
            s_aq.n += 16 * HPC
            for g in range(64):
                sdv[g % 2].n += 16
                sdb[g % 2].n += 16
                sob[g % 2].n += 16

        phase_fence()

        phD = contextlib.ExitStack()
        with phD:
            abuf = phD.enter_context(
                nc.sbuf_tensor("abuf", [128, NCH * NT], BF16))
            xin = [phD.enter_context(nc.sbuf_tensor(f"xiD{i}", [128, NT], F32))
                   for i in range(3)]
            xo_sb = [phD.enter_context(nc.sbuf_tensor(f"xqD{i}", [128, NT], F32))
                     for i in range(2)]
            wslab = [phD.enter_context(
                nc.sbuf_tensor(f"wbD{i}", [128, NCH * 128], BF16))
                for i in range(2)]
            ps_m = [phD.enter_context(nc.psum_tensor(f"pDm{i}", [128, NT], F32))
                    for i in range(2)]

            aooA_v = aooA.rearrange("f (tb t) -> (f tb) t", tb=4)
            aooB_v = aooB.rearrange("f (tb t) -> (f tb) t", tb=4)
            gl1 = c_gl.n
            mmD = c_mm.n
            swlD = slot_bases(swl)

            @block.gpsimd
            def _(g):
                for m in range(2):
                    g.dma_start(
                        out=wslab[m % 2][:, :].rearrange("p (c f) -> p c f",
                                                         c=NCH),
                        in_=woutT[:, 128 * m:128 * (m + 1)].rearrange(
                            "(c p) f -> p c f", p=128)
                    ).then_inc(swl[m % 2].sem, 16)
                g.wait_ge(c_cc.sem, T_AGO1)
                for kc in range(NCH):
                    if kc == 28:
                        g.wait_ge(c_cc.sem, T_AGO2)
                    g.indirect_dma_start(
                        out=abuf[:, NT * kc:NT * (kc + 1)],
                        out_offset=None, in_=(aooA_v if kc < 28 else aooB_v),
                        in_offset=IndirectOffsetOnAxis(
                            ap=oidx_sb[:, kc:kc + 1], axis=0)) \
                        .then_inc(c_gl.sem, 16)
                for m in range(2, NCH):
                    if m >= 2:
                        g.wait_ge(c_mm.sem, mmD + m - 1)
                    g.dma_start(
                        out=wslab[m % 2][:, :].rearrange("p (c f) -> p c f",
                                                         c=NCH),
                        in_=woutT[:, 128 * m:128 * (m + 1)].rearrange(
                            "(c p) f -> p c f", p=128)
                    ).then_inc(swl[m % 2].sem, 16)
            c_gl.n += 16 * NCH
            slot_bump(swl, 2, NCH)

            evD = c_ev.n
            sxbD = slot_bases(sx)
            ssdD = slot_bases(ssd)

            @block.tensor
            def _(t):
                t.wait_ge(c_gl.sem, gl1 + 16 * NCH)
                for m in range(NCH):
                    t.wait_ge(*slot_tok(swl, swlD, 2, m))
                    if m >= 2:
                        t.wait_ge(c_ev.sem, evD + m - 1)
                    for kc in range(NCH):
                        for hf in range(2):
                            last = t.matmul(
                                ps_m[m % 2][:, 512 * hf:512 * (hf + 1)],
                                wslab[m % 2][:, 128 * kc:128 * (kc + 1)],
                                abuf[:, NT * kc + 512 * hf:
                                     NT * kc + 512 * (hf + 1)],
                                start=(kc == 0), stop=(kc == NCH - 1))
                    last.then_inc(c_mm.sem, 1)
            c_mm.n += NCH

            @block.sync
            def _(s):
                for m in range(NCH + 2):
                    if m < NCH:
                        if m >= 3:
                            s.wait_ge(c_ev.sem, evD + m - 2)
                        s.dma_start(out=xin[m % 3][:, :],
                                    in_=xres1[128 * m:128 * (m + 1), :]) \
                            .then_inc(sx[m % 3].sem, 16)
                    if m >= 2:
                        s.wait_ge(c_ev.sem, evD + m - 1)
                        s.dma_start(out=xres2[128 * (m - 2):128 * (m - 1), :],
                                    in_=xo_sb[m % 2][:, :]) \
                            .then_inc(ssd[m % 2].sem, 16)
            slot_bump(sx, 3, NCH)
            slot_bump(ssd, 2, NCH)

            @block.vector
            def _(v):
                for m in range(NCH):
                    v.wait_ge(c_mm.sem, mmD + m + 1)
                    v.wait_ge(*slot_tok(sx, sxbD, 3, m))
                    if m >= 2:
                        v.wait_ge(*slot_tok(ssd, ssdD, 2, m - 2))
                    v.tensor_add(xo_sb[m % 2][:, :], ps_m[m % 2][:, :],
                                 xin[m % 3][:, :]).then_inc(c_ev.sem, 1)
            c_ev.n += NCH



        phase_fence()

        emit_ffn(xres2, outT, 1, False, "e")

        fin = [(s.sem, s.n) for s in ssd]

        @block.sync
        def _(s):
            for sm, vv in fin:
                s.wait_ge(sm, vv)

        @block.gpsimd
        def _(g):
            for sm, vv in fin:
                g.wait_ge(sm, vv)

        nc._dbg_counts = {c.sem.name if hasattr(c.sem, 'name') else str(i):
                          (c.sem, c.n) for i, c in enumerate(ALL_CTRS)}
    return nc



_CACHE = {}


def _permute_wout(w):
    order = [8 * (c // 7) + c % 7 for c in range(28)] + \
        [8 * c + 7 for c in range(4)]
    blocks = [w[128 * H:128 * (H + 1), :] for H in order]
    return np.ascontiguousarray(np.concatenate(blocks, axis=0))


def _prep(inp):
    f32 = lambda a: np.asarray(a, dtype=np.float32)
    x = np.ascontiguousarray(inp["L0_attn_hidden_states"], dtype=np.float32)
    wqT = np.ascontiguousarray(f32(inp["L0_attn_w_qkv"]).T)
    qln = tuple(f32(inp[k]) for k in ("L0_attn_q_ln_w", "L0_attn_q_ln_b",
                                      "L0_attn_k_ln_w", "L0_attn_k_ln_b"))
    shared = {
        "woutT": _permute_wout(np.ascontiguousarray(
            f32(inp["L0_attn_w_out"]).T)),
        "ffnWT": np.ascontiguousarray(f32(inp["L0_ffn_weight"]).T),
        "loraAT": np.ascontiguousarray(
            f32(inp["L0_ffn_lora_A"]).transpose(1, 0, 2).reshape(D, E * R)),
        "loraBs": np.ascontiguousarray(
            f32(inp["L0_ffn_lora_B"]).reshape(E * R, D)),
        "gw1T": np.ascontiguousarray(f32(inp["L0_ffn_gw1"]).T),
        "gw2T": np.ascontiguousarray(f32(inp["L0_ffn_gw2"]).T),
        "gvec": np.ascontiguousarray(np.stack(
            [f32(inp["L0_ffn_gb1"]), f32(inp["L0_ffn_ln_gamma"]),
             f32(inp["L0_ffn_ln_beta"])], axis=1)),
        "gb2r": np.ascontiguousarray(f32(inp["L0_ffn_gb2"]).reshape(1, E)),
    }
    return x, wqT, qln, shared


def _build_inputs(core, x, wqT, qln, shared):
    b, j = core // 4, core % 4
    m = dict(shared)
    m["xT"] = np.ascontiguousarray(
        x[b, NT * j:NT * (j + 1), :].T).astype(np.float32, copy=False)
    sl = slice(NT * j, NT * (j + 1))
    m["wqkvT"] = np.ascontiguousarray(np.concatenate(
        [wqT[:, D + NT * j:D + NT * (j + 1)],
         wqT[:, NT * j:NT * (j + 1)],
         wqT[:, 2 * D + NT * j:2 * D + NT * (j + 1)]], axis=1))
    qkln = np.zeros((128, 32), np.float32)
    for r in range(8):
        for c4 in range(4):
            qkln[:, 4 * r + c4] = qln[c4][NT * j + 128 * r:
                                          NT * j + 128 * (r + 1)]
    m["qkln"] = qkln
    p = np.arange(128)
    alibik = np.zeros((128, HPC * NCH), np.float32)
    alibiq = np.zeros((1, HPC * S), np.float32)
    for hh in range(HPC):
        h = 8 * j + hh
        slope = 2.0 ** (-(h + 1) / 4.0)
        for kc in range(NCH):
            alibik[:, NCH * hh + kc] = slope * (128 * kc + p)
        alibiq[0, S * hh:S * (hh + 1)] = \
            (slope * np.arange(S) + 1.0) / SCALE
    oidx = np.zeros((128, NCH), np.int32)
    for kc in range(NCH):
        c_ = kc if kc < 28 else kc - 28
        oidx[:, kc] = 4 * (128 * c_ + p) + j
    m["oidx"] = oidx
    m["alibik"] = alibik
    m["alibiq"] = alibiq
    return m


def kernel(L0_attn_hidden_states, L0_attn_w_qkv, L0_attn_q_ln_w,
           L0_attn_q_ln_b, L0_attn_k_ln_w, L0_attn_k_ln_b, L0_attn_w_out,
           L0_ffn_weight, L0_ffn_lora_A, L0_ffn_lora_B, L0_ffn_gw1,
           L0_ffn_gb1, L0_ffn_ln_gamma, L0_ffn_ln_beta, L0_ffn_gw2,
           L0_ffn_gb2):
    inp = dict(
        L0_attn_hidden_states=L0_attn_hidden_states,
        L0_attn_w_qkv=L0_attn_w_qkv, L0_attn_q_ln_w=L0_attn_q_ln_w,
        L0_attn_q_ln_b=L0_attn_q_ln_b, L0_attn_k_ln_w=L0_attn_k_ln_w,
        L0_attn_k_ln_b=L0_attn_k_ln_b, L0_attn_w_out=L0_attn_w_out,
        L0_ffn_weight=L0_ffn_weight, L0_ffn_lora_A=L0_ffn_lora_A,
        L0_ffn_lora_B=L0_ffn_lora_B, L0_ffn_gw1=L0_ffn_gw1,
        L0_ffn_gb1=L0_ffn_gb1, L0_ffn_ln_gamma=L0_ffn_ln_gamma,
        L0_ffn_ln_beta=L0_ffn_ln_beta, L0_ffn_gw2=L0_ffn_gw2,
        L0_ffn_gb2=L0_ffn_gb2)
    x, wqT, qln, shared = _prep(inp)
    if "nc" not in _CACHE:
        _CACHE["nc"] = build_program()
    nc = _CACHE["nc"]
    in_maps = [_build_inputs(c, x, wqT, qln, shared) for c in range(8)]
    res = run_bass_kernel_spmd(nc, in_maps, core_ids=list(range(8)))
    out = np.empty((2, S, D), np.float32)
    for c in range(8):
        b, j = c // 4, c % 4
        out[b, NT * j:NT * (j + 1), :] = res.results[c]["outT"].T
    return out
